# version 58
# baseline (speedup 1.0000x reference)
import contextlib
import numpy as np
import concourse.bass as bass
import concourse.mybir as mybir
from concourse.bass_utils import run_bass_kernel_spmd

F32 = mybir.dt.float32
F16 = mybir.dt.float16
ALU = mybir.AluOpType
AF = mybir.ActivationFunctionType
AX = mybir.AxisListType

NCORES = 8
WT_STAT = [0, 0]
BLOC = 2
T = 2048
D = 1024
NST = 16
TC = 512
NCHB = T // TC
DCH = D // 128
NTT = TC // 128
EPS = 1e-5
DAR = 8
DA_HEAD = 16
HR = 8
WRING = 4

C_MM = 0.27
C_TR = 0.15
SCAN_NS = 1.30


class Chain:
    def __init__(self, sem):
        self.sem = sem
        self.n = 0

    def __call__(self, inst):
        inst.then_inc(self.sem, 1)
        self.n += 1
        return self.n


def build_core(M_in, M_out):
    nch = BLOC * NCHB
    nc = bass.Bass("TRN2", target_bir_lowering=False)
    dr = {}
    dr["x"] = nc.dram_tensor("x", [BLOC, T, D], F32, kind="ExternalInput").ap()
    dr["w_in"] = nc.dram_tensor("w_in", [2 * D, D], F32, kind="ExternalInput").ap()
    dr["b_in"] = nc.dram_tensor("b_in", [2 * D], F32, kind="ExternalInput").ap()
    dr["w_x"] = nc.dram_tensor("w_x", [NST + 2 * D, D], F32, kind="ExternalInput").ap()
    dr["b_x"] = nc.dram_tensor("b_x", [NST + 2 * D], F32, kind="ExternalInput").ap()
    dr["w_dt"] = nc.dram_tensor("w_dt", [D, NST], F32, kind="ExternalInput").ap()
    dr["b_dt"] = nc.dram_tensor("b_dt", [D], F32, kind="ExternalInput").ap()
    dr["a_log"] = nc.dram_tensor("a_log", [D, NST], F32, kind="ExternalInput").ap()
    dr["d_par"] = nc.dram_tensor("d_par", [D], F32, kind="ExternalInput").ap()
    dr["w_out"] = nc.dram_tensor("w_out", [D, D], F32, kind="ExternalInput").ap()
    dr["b_out"] = nc.dram_tensor("b_out", [D], F32, kind="ExternalInput").ap()
    g1r_h = nc.dram_tensor("g1r", [1, 32, 4, 32], F32, kind="ExternalInput")
    g2r_h = nc.dram_tensor("g2r", [4, 16, 1, 16], F32, kind="ExternalInput")
    g1i_h = nc.dram_tensor("g1i", [1, 32, 4, 32], F32, kind="ExternalInput")
    g2i_h = nc.dram_tensor("g2i", [4, 16, 1, 16], F32, kind="ExternalInput")
    dr["bfr"] = nc.dram_tensor("bfr", [512], F32, kind="ExternalInput").ap()
    dr["bfi"] = nc.dram_tensor("bfi", [512], F32, kind="ExternalInput").ap()
    dr["y"] = nc.dram_tensor("y", [BLOC, T, D], F32, kind="ExternalOutput").ap()
    w4 = {"r": nc.dram_tensor("w4r", [1024, 256], F16),
          "i": nc.dram_tensor("w4i", [1024, 256], F16)}
    wt_in2 = nc.dram_tensor("wt_in2", [16, 128, DCH * 128], F16)
    wt_x2 = nc.dram_tensor("wt_x2", [17, 128, DCH * 128], F16)
    GREC = 128 * DCH * 128

    es = contextlib.ExitStack()
    with es:
        def sb(name, shape, dtype):
            return es.enter_context(nc.sbuf_tensor(name, shape, dtype))

        def pst(name, shape, dtype):
            return es.enter_context(nc.psum_tensor(name, shape, dtype))

        def mksem(name):
            return es.enter_context(nc.semaphore(name))

        w_dt = sb("wdt", [NST, D], F16)
        w_out = [sb(f"wo{k}", [128, D], F16) for k in range(DCH)]
        w_ffn = [sb(f"wf{k}", [128, D], F16) for k in range(DCH)]
        rin = [sb(f"rin{s}", [128, DCH * 128], F16) for s in range(WRING)]
        rx = [sb(f"rx{s}", [128, DCH * 128], F16) for s in range(WRING)]
        bin_c = sb("bin_c", [128, 16], F32)
        bdlt = sb("bdlt", [NST, 1], F32)
        bB = sb("bB", [128, DCH], F32)
        bC = sb("bC", [128, DCH], F32)
        bdt = sb("bdt", [128, DCH], F32)
        bcomb = sb("bcomb", [128, DCH], F32)
        bffn = sb("bffn", [128, DCH], F32)
        a_cols = [sb(f"ac{k}", [128, NST], F32) for k in range(DCH)]
        d_cols = sb("d_cols", [128, DCH], F32)
        eye16 = sb("eye16", [128, 128], F16)
        eps_c = sb("eps_c", [128, 1], F32)
        mhalf_c = sb("mhalf_c", [128, 1], F32)
        x_t = [sb(f"x_t{i}", [128, D], F32) for i in range(2)]
        xr_t = [sb(f"xr_t{i}", [128, D], F32) for i in range(2)]
        xn_tm = sb("xn_tm", [128, D], F16)
        otm16 = sb("otm16", [128, D], F16)
        out_tm = [sb(f"out_tm{i}", [128, D], F32) for i in range(2)]
        xn_f = [sb(f"xnf{p}", [128, DCH * TC], F16) for p in range(2)]
        xin = [[sb(f"xin{p}_{k}", [128, TC], F16) for k in range(DCH)]
               for p in range(2)]
        sres = [[sb(f"sres{p}_{k}", [128, TC], F16) for k in range(DCH)]
                for p in range(2)]
        Bu = [[sb(f"Bu{p}_{k}", [128, TC], F16) for k in range(DCH)]
              for p in range(2)]
        Cf = [[sb(f"Cf{p}_{k}", [128, TC], F16) for k in range(DCH)]
              for p in range(2)]
        dlt = [[sb(f"dlt{p}_{k}", [128, TC], F16) for k in range(DCH)]
               for p in range(2)]
        ao = [sb(f"ao{k}", [128, TC], F16) for k in range(DCH)]
        dlt16 = sb("dlt16", [NST, TC], F16)
        da_r = [sb(f"da{r}", [128, TC], F16) for r in range(DAR)]
        h_buf = sb("h_buf", [128, HR * TC], F16)
        carry = [sb(f"cr{k}", [128, NST], F16) for k in range(DCH)]
        y_s = sb("y_s", [128, TC], F32)
        y2_s = sb("y2_s", [128, TC], F32)
        sumx = sb("sumx", [128, NTT], F32)
        sumsq = sb("sumsq", [128, NTT], F32)
        mcol = sb("mcol", [128, NTT], F32)
        vcol = sb("vcol", [128, NTT], F32)
        rstd = sb("rstd", [128, NTT], F32)
        nmr = sb("nmr", [128, NTT], F32)
        wst16 = [rx[2], rx[3]]
        wstT = [rin[0], rin[1], rin[2], rin[3]]
        LSTG = [x_t[0], x_t[1], xr_t[0], xr_t[1], out_tm[1]]

        pT1 = pst("pT1", [128, 1024], F16)
        pM1 = pst("pM1", [128, 512], F32)
        pM2 = pst("pM2", [128, 512], F32)
        pS = pst("pS", [128, 512], F32)
        pS2 = pst("pS2", [128, 512], F32)
        pO1 = pst("pO1", [128, 512], F32)
        pO2 = pst("pO2", [128, 512], F32)
        pTo = pst("pTo", [128, 1024], F16)

        dma_names = ("ldx0 ldx1 ldr0 ldr1 rw0 rw1 rw2 rw3 rw4 pwf pd2 sto "
                     "wdt0 wdt1 wdt2 wdt3 "
                     + " ".join(f"ri{s}" for s in range(WRING)) + " "
                     + " ".join(f"rb{s}" for s in range(WRING))).split()
        DS = {n: mksem(n) for n in dma_names}
        TCH = Chain(mksem("tch"))
        VCH = Chain(mksem("vch"))
        ACH = Chain(mksem("ach"))
        GCH = Chain(mksem("gch"))
        CH = {"t": TCH, "v": VCH, "a": ACH, "g": GCH}

        def mark(key, chain):
            M_out[key] = chain.n

        wait_hi = {}
        wt_stat = WT_STAT
        wt_stat[0] = wt_stat[1] = 0

        def wt(eng, chain_id, key):
            val = M_in.get(key, 0)
            if val > 0:
                wt_stat[0] += 1
                hk = (id(eng), chain_id)
                if wait_hi.get(hk, 0) >= val:
                    wt_stat[1] += 1
                    return
                wait_hi[hk] = val
                eng.wait_ge(CH[chain_id].sem, val)

        N_FIXED = 21
        ROW_JOBS = []
        for r in range(16):
            ROW_JOBS.append(("w_in", r * 128, 128, 1024))
        for r in range(16):
            ROW_JOBS.append(("w_x", r * 128, 128, 1024))
        ROW_JOBS.append(("w_x", 2048, 16, 1024))
        for r in range(8):
            ROW_JOBS.append(("w_out", r * 128, 128, 1024))
        for r in range(8):
            ROW_JOBS.append(("w_dt", r * 128, 128, 16))
        NJOBS = len(ROW_JOBS)

        def xp_cols(m2):
            if m2 == 0:
                return 0, 16
            if m2 <= 8:
                return 16 + (m2 - 1) * 128, 16 + m2 * 128
            return 1040 + (m2 - 9) * 128, 1040 + (m2 - 8) * 128

        load_ord = {}
        _lcnt = {}
        for _c in range(nch):
            for _gi in range(33):
                _gidx = _c * 33 + _gi
                _slot = _gidx % WRING
                _sem = ("ri" if _gi < 16 else "rb") + str(_slot)
                _lcnt[_sem] = _lcnt.get(_sem, 0) + 1
                load_ord[(_c, _gi)] = (_sem, _lcnt[_sem])

        def h_sl(gn):
            r = gn % HR
            return h_buf[:, r * TC:(r + 1) * TC]

        wdt_cnt = {0: 0, 1: 0, 2: 0, 3: 0}
        wdt_after = {}

        blk = es.enter_context(nc.Block())

        def c_(tile, i):
            return tile[:, i:i + 1]

        BNDS = {}
        for w_ in range(nch + 2):
            cs_ = []
            if w_ >= 2:
                cs_ += [4.32] * 8 + [1.2] * 4
            if w_ < nch:
                cs_ += ([5.6] * 4 + [2.16] * 16 + [2.16] * 9 + [1.2] * 8
                        + [2.16] * 8)
            b_ = [0.0]
            for c_v in cs_:
                b_.append(b_[-1] + c_v)
            BNDS[w_] = b_


        @blk.sync
        def _(sy):
            ctx = nc.allow_non_contiguous_dma(reason="one-time small loads")
            ctx.__enter__()
            sy.dma_start(out=bin_c[:, :],
                         in_=dr["b_in"].rearrange("(m p) -> p m", p=128)).then_inc(DS["pwf"], 16)
            sy.dma_start(out=bdlt[:, :],
                         in_=dr["b_x"][0:NST].unsqueeze(1)).then_inc(DS["pwf"], 16)
            sy.dma_start(out=bB[:, :],
                         in_=dr["b_x"][NST:NST + D].rearrange("(m p) -> p m", p=128)).then_inc(DS["pwf"], 16)
            sy.dma_start(out=bC[:, :],
                         in_=dr["b_x"][NST + D:].rearrange("(m p) -> p m", p=128)).then_inc(DS["pwf"], 16)
            sy.dma_start(out=bdt[:, :],
                         in_=dr["b_dt"].rearrange("(m p) -> p m", p=128)).then_inc(DS["pwf"], 16)
            sy.dma_start(out=bcomb[:, :],
                         in_=dr["b_out"].rearrange("(m p) -> p m", p=128)).then_inc(DS["pwf"], 16)
            sy.dma_start(out=bffn[:, 0:4],
                         in_=dr["bfr"].rearrange("(m p) -> p m", p=128)).then_inc(DS["pwf"], 16)
            sy.dma_start(out=bffn[:, 4:8],
                         in_=dr["bfi"].rearrange("(m p) -> p m", p=128)).then_inc(DS["pwf"], 16)
            sy.dma_start(out=d_cols[:, :],
                         in_=dr["d_par"].rearrange("(m p) -> p m", p=128)).then_inc(DS["pwf"], 16)
            for k in range(DCH):
                sy.dma_start(out=a_cols[k][:, :],
                             in_=dr["a_log"][k * 128:(k + 1) * 128, :]).then_inc(DS["pwf"], 16)
            for gh, off1 in ((g1r_h, 0), (g1i_h, 32 * 1024)):
                sy.dma_start(
                    out=bass.AP(out_tm[0], off1, [[1024, 4], [32, 32], [1, 32]]),
                    in_=bass.AP(gh, 0, [[32, 4], [128, 32], [1, 32]]),
                ).then_inc(DS["pwf"], 16)
            for gh, off2 in ((g2r_h, 0), (g2i_h, 32 * 512)):
                sy.dma_start(
                    out=bass.AP(y_s, off2, [[512, 4], [1, 256]]),
                    in_=bass.AP(gh, 0, [[256, 4], [1, 256]]),
                ).then_inc(DS["pwf"], 16)
            ctx.__exit__(None, None, None)
            wbctx = nc.allow_non_contiguous_dma(reason="wb group splits")
            wbctx.__enter__()

            def emit_wb(j):
                wname, r0, nr, ncol = ROW_JOBS[j]
                wt(sy, "v", ("wevac", j))
                if wname == "w_in":
                    dst = bass.AP(wt_in2, (r0 // 128) * GREC,
                                  [[1024, 128], [1, 1024]])
                    src = bass.AP(wstT[j % 4], 0, [[1024, 128], [1, 1024]])
                    sy.dma_start(out=dst, in_=src).then_inc(
                        DS["wdt" + str(j % 4)], 16)
                    wdt_cnt[j % 4] += 1
                else:
                    segs = []
                    R = r0
                    while R < r0 + nr:
                        if R < 16:
                            g, qg, ncg = 0, R, 16
                        else:
                            g = 1 + (R - 16) // 128
                            qg = (R - 16) % 128
                            ncg = 128
                        qlen = min(r0 + nr - R, ncg - qg)
                        segs.append((g, qg, R - r0, qlen, ncg))
                        R += qlen
                    for (g, qg, jq0, qlen, ncg) in segs:
                        dst = bass.AP(wt_x2, g * GREC + qg,
                                      [[1024, 128], [ncg, DCH], [1, qlen]])
                        src = bass.AP(wstT[j % 4], jq0,
                                      [[1024, 128], [128, DCH], [1, qlen]])
                        sy.dma_start(out=dst, in_=src).then_inc(
                            DS["wdt" + str(j % 4)], 16)
                        wdt_cnt[j % 4] += 1
                wdt_after[j] = wdt_cnt[j % 4]

            for j, (wname, r0, nr, ncol) in enumerate(ROW_JOBS):
                if j % 2 == 0:
                    if j >= 5:
                        wt(sy, "v", ("cast", j - 5))
                    stg = LSTG[j % 5]
                    sy.dma_start(out=stg[0:nr, 0:ncol],
                                 in_=dr[wname][r0:r0 + nr, :]
                                 ).then_inc(DS["rw" + str(j % 5)], 16)
                if 1 <= j and j - 1 < 33:
                    emit_wb(j - 1)
            def emit_store(g):
                t_ = g % NTT
                b2, ci2 = divmod(g // NTT, NCHB)
                wt(sy, "g", ("oadd", g))
                if g >= 1:
                    sy.wait_ge(DS["sto"], 16 * g)
                sy.dma_start(
                    out=dr["y"][b2, ci2 * TC + t_ * 128:ci2 * TC + (t_ + 1) * 128, :],
                    in_=out_tm[g % 2][:, :]).then_inc(DS["sto"], 16)

            for cP in range(nch):
                b, ci = divmod(cP, NCHB)
                store_t = [0]

                def maybe_store(s_now, cP=cP, store_t=store_t):
                    while (cP >= 2 and store_t[0] < NTT
                           and 16 * store_t[0] + 20 <= s_now):
                        emit_store((cP - 2) * NTT + store_t[0])
                        store_t[0] += 1

                maybe_store(36)
                for t in range(NTT):
                    g = cP * NTT + t
                    if g >= 2:
                        wt(sy, "a", ("apply", g - 2))
                    else:
                        wt(sy, "v", ("cast", NJOBS - 1))
                    sy.dma_start(
                        out=x_t[g % 2][:, :],
                        in_=dr["x"][b, ci * TC + t * 128:ci * TC + (t + 1) * 128, :],
                    ).then_inc(DS["ldx" + str(g % 2)], 16)
                for gi in range(33):
                    gidx = cP * 33 + gi
                    slot = gidx % WRING
                    if gidx >= WRING:
                        wt(sy, "t", ("wgrp", gidx - WRING))
                    else:
                        for _s in range(4):
                            sy.wait_ge(DS["wdt" + str(_s)],
                                       16 * wdt_cnt[_s])
                    if gi == 16:
                        wt(sy, "a", ("stats", cP * 4 + 3))
                    if cP >= 1:
                        off = 12 if cP >= 2 else 0
                        if gi < 16:
                            stp = off + 4 + gi
                        elif gi - 16 <= 8:
                            stp = off + 20 + (gi - 16)
                        else:
                            stp = off + 29 + 8 + (gi - 16 - 9)
                        s_idx = int((BNDS[cP][stp] - 25.0) / SCAN_NS) - 3
                        if s_idx > 0:
                            wt(sy, "v",
                               ("scan", (cP - 1) * 128 + min(127, s_idx)))
                        maybe_store(s_idx)
                    if cP == 0 and gi >= 16 and slot in (2, 3):
                        wt(sy, "t", ("wtrans", NJOBS - 1))
                    if gi < 16:
                        src = bass.AP(wt_in2, gi * GREC,
                                      [[1024, 128], [1, 1024]])
                        dst = bass.AP(rin[slot], 0, [[1024, 128], [1, 1024]])
                    else:
                        cc0, cc1 = xp_cols(gi - 16)
                        width = DCH * (cc1 - cc0)
                        src = bass.AP(wt_x2, (gi - 16) * GREC,
                                      [[1024, 128], [1, width]])
                        dst = bass.AP(rx[slot], 0, [[1024, 128], [1, width]])
                    sem = DS[load_ord[(cP, gi)][0]]
                    sy.dma_start(out=dst, in_=src).then_inc(sem, 16)
                maybe_store(999)
            for g in range(max(0, nch - 2) * NTT, nch * NTT):
                emit_store(g)

        @blk.gpsimd
        def _(gp):
            GCH(gp.iota(y2_s[:, 0:128], [[1, 128]], channel_multiplier=0,
                        allow_small_or_imprecise_dtypes=True))
            GCH(gp.iota(y2_s[:, 128:256], [[0, 128]], channel_multiplier=1,
                        allow_small_or_imprecise_dtypes=True))
            GCH(gp.memset(eps_c[:, :], EPS))
            GCH(gp.memset(mhalf_c[:, :], -0.5))
            mark(("prep_gp",), GCH)

            def emit_w4():
                w4v3 = xr_t[0][:, :].bitcast(F16).rearrange("p (r e) -> p r e", r=8)
                wt(gp, "v", ("w4ev", 0))
                gp.dma_start(out=bass.AP(w4["r"], 0, [[256, 128], [32768, 8], [1, 256]]),
                             in_=w4v3).then_inc(DS["pd2"], 16)
                wt(gp, "v", ("w4ev", 1))
                gp.dma_start(out=bass.AP(w4["i"], 0, [[256, 128], [32768, 8], [1, 256]]),
                             in_=w4v3).then_inc(DS["pd2"], 16)
                gp.wait_ge(DS["pd2"], 32)
                for kk in range(4):
                    for (src, dstk, colb) in (("r", kk, 0), ("i", kk, 512),
                                              ("i", kk + 4, 0), ("r", kk + 4, 512)):
                        in_ap = bass.AP(w4[src], kk * 2048,
                                        [[16, 128], [8192, 32], [1, 16]])
                        gp.dma_start(out=w_ffn[dstk][:, colb:colb + 512],
                                     in_=in_ap).then_inc(DS["pd2"], 16)
            for w in range(nch + 2):
                cO, cS, cP = w - 2, w - 1, w
                ev = []
                if w == 1:
                    ev.append((18.5, "w4", 0))
                if cP < nch:
                    off = 12 if w >= 2 else 0
                    for t in range(NTT):
                        ev.append((BNDS[w][off + t] + 1.0, "ln", t))
                    dtb = BNDS[w][-17]
                    for j in range(DCH // 2):
                        ev.append((dtb + 2.0 + 5.0 * j, "dfin", 2 * j))
                if 0 <= cS < nch:
                    for k in range(DCH):
                        ev.append(((16 * k + 18) * SCAN_NS, "attn", k))
                if cO >= 0:
                    for t in range(NTT):
                        ev.append((36.0 + 2.0 * t, "oadd", t))
                ev.sort()
                for _rd, kind, idx in ev:
                    if kind == "w4":
                        emit_w4()
                    elif kind == "ln":
                        t = idx
                        g = cP * NTT + t
                        wt(gp, "a", ("stats", g))
                        s3 = GCH(gp.tensor_scalar(c_(mcol, t), c_(sumx, t),
                                                  1.0 / D, None, op0=ALU.mult))
                        GCH(gp.tensor_scalar(c_(vcol, t), c_(sumsq, t),
                                             1.0 / D, None, op0=ALU.mult))
                        gp.wait_ge(GCH.sem, s3)
                        s4 = GCH(gp.tensor_mul(c_(nmr, t), c_(mcol, t),
                                               c_(mcol, t)))
                        gp.wait_ge(GCH.sem, s4)
                        GCH(gp.tensor_sub(c_(vcol, t), c_(vcol, t), c_(nmr, t)))
                        mark(("lnv", g), GCH)
                        wt(gp, "a", ("rstd", g))
                        GCH(gp.tensor_scalar(c_(nmr, t), c_(mcol, t),
                                             c_(rstd, t), -1.0, op0=ALU.mult,
                                             op1=ALU.mult))
                        mark(("nm", g), GCH)
                    elif kind == "attn":
                        k = idx
                        P = cS % 2
                        wt(gp, "v", ("epi2", cS * 8 + k))
                        GCH(gp.tensor_mul(Bu[P][k][:, :], y2_s[:, :],
                                          sres[P][k][:, :]))
                        mark(("attn", cS * 8 + k), GCH)
                    elif kind == "oadd":
                        t = idx
                        g = cO * NTT + t
                        b, ci = divmod(cO, NCHB)
                        if g >= 2:
                            wt(gp, "g", ("oadd", g - 2))
                        else:
                            gp.wait_ge(DS["pd2"], 32)
                        gp.dma_start(
                            out=xr_t[g % 2][:, :],
                            in_=dr["x"][b, ci * TC + t * 128:ci * TC + (t + 1) * 128, :],
                        ).then_inc(DS["ldr" + str(g % 2)], 16)
                        wt(gp, "a", ("pto", g))
                        gp.wait_ge(DS["ldr" + str(g % 2)], (g // 2 + 1) * 16)
                        if g >= 2:
                            gp.wait_ge(DS["sto"], (g - 1) * 16)
                        GCH(gp.tensor_add(out_tm[g % 2][:, :], otm16[:, :],
                                          xr_t[g % 2][:, :]))
                        mark(("oadd", g), GCH)
                    else:
                        k = idx
                        P = cP % 2
                        wt(gp, "a", ("dln", cP * 8 + k + 1))
                        cp0 = GCH(gp.tensor_scalar(dlt[P][k][:, :],
                                                   dlt[P][k][:, :], 0.1, 1e-5,
                                                   op0=ALU.min, op1=ALU.max))
                        cp1 = GCH(gp.tensor_scalar(dlt[P][k + 1][:, :],
                                                   dlt[P][k + 1][:, :], 0.1,
                                                   1e-5, op0=ALU.min,
                                                   op1=ALU.max))
                        gp.wait_ge(GCH.sem, cp0)
                        m0 = GCH(gp.tensor_mul(Bu[P][k][:, :], Bu[P][k][:, :],
                                               dlt[P][k][:, :]))
                        gp.wait_ge(GCH.sem, cp1)
                        m1 = GCH(gp.tensor_mul(Bu[P][k + 1][:, :],
                                               Bu[P][k + 1][:, :],
                                               dlt[P][k + 1][:, :]))
                        gp.wait_ge(GCH.sem, m0)
                        GCH(gp.tensor_mul(Bu[P][k][:, :], Bu[P][k][:, :],
                                          xin[P][k][:, :]))
                        mark(("u", cP * 8 + k), GCH)
                        mark(("dclip", cP * 8 + k), GCH)
                        gp.wait_ge(GCH.sem, m1)
                        GCH(gp.tensor_mul(Bu[P][k + 1][:, :],
                                          Bu[P][k + 1][:, :],
                                          xin[P][k + 1][:, :]))
                        mark(("u", cP * 8 + k + 1), GCH)
                        mark(("dclip", cP * 8 + k + 1), GCH)
            gp.wait_ge(DS["sto"], nch * NTT * 16)

        @blk.tensor
        def _(te):
            wt(te, "v", ("eye",))
            for j, (wname, r0, nr, ncol) in enumerate(ROW_JOBS):
                wt(te, "v", ("cast", j))
                if j >= 1:
                    wt(te, "v", ("wevac", j - 1))
                if wname == "w_dt":
                    TCH(te.transpose(pT1[0:16, 0:128],
                                     wst16[j % 2][0:128, 0:16], eye16[:, :]))
                elif nr == 128:
                    for k in range(DCH):
                        TCH(te.transpose(pT1[:, k * 128:(k + 1) * 128],
                                         wst16[j % 2][:, k * 128:(k + 1) * 128],
                                         eye16[:, :]))
                else:
                    for k in range(DCH):
                        TCH(te.transpose(pT1[:, k * 128:k * 128 + 16],
                                         wst16[j % 2][0:16, k * 128:(k + 1) * 128],
                                         eye16[0:16, 0:16]))
                mark(("wtrans", j), TCH)
            te.wait_ge(DS["pwf"], 16 * N_FIXED)

            def emit_ffmm4(t_):
                for gidx in range(4 * t_, 4 * t_ + 4):
                    half, m = divmod(gidx, 8)
                    if gidx >= 2:
                        wt(te, "v", ("ffev", gidx - 2))
                    else:
                        wt(te, "a", ("xpev", 13 + gidx))
                    bank = [pO1, pO2][gidx % 2]
                    TCH(te.matmul(
                        bank[:, 0:256],
                        bass.AP(out_tm[0], half * 32 * 1024 + m * 128, [[1024, 4], [1, 128]]),
                        bass.AP(y_s, half * 32 * 512, [[512, 4], [1, 16], [16, 16]]),
                        start=True, stop=True))
                    mark(("ffmm", gidx), TCH)
            for w in range(nch + 2):
                cO, cS, cP = w - 2, w - 1, w
                sq = list(range(128)) if 0 <= cS < nch else []
                sqi = [0]
                sk = [0]
                bnd = BNDS[w]

                def emit_sigma(gn, cS=cS):
                    k, n = gn // NST, gn % NST
                    sbank = [pS, pS2][k % 2]
                    wt(te, "v", ("scan", cS * 128 + gn))
                    if n == 0 and cS * 8 + k >= 2:
                        wt(te, "v", ("epiy", cS * 8 + k - 2))
                    TCH(te.matmul(sbank[:, :], eye16[:, :], h_sl(gn),
                                  start=(n == 0), stop=(n == NST - 1)))
                    mark(("summ", cS * 128 + gn), TCH)

                def pump(_cost=None):
                    B = bnd[sk[0]]
                    sk[0] += 1
                    while sqi[0] < len(sq):
                        gn = sq[sqi[0]]
                        if (gn - 3) * SCAN_NS > B:
                            break
                        emit_sigma(gn)
                        sqi[0] += 1

                if cO >= 0:
                    P = cO % 2
                    for m in range(8):
                        bank = [pO1, pO2][m % 2]
                        pump()
                        if m == 0:
                            wt(te, "g", ("attn", cO * 8 + 7))
                        if m < 2:
                            wt(te, "a", ("xpev", (cO + 1) * 17 + 13 + m))
                        else:
                            wt(te, "a", ("aoev", cO * 8 + m - 2))
                        if cO == 0 and m < 2:
                            wt(te, "v", ("wneg",))
                        for k in range(DCH):
                            TCH(te.matmul(bank[:, :],
                                          w_out[k][:, m * 128:(m + 1) * 128],
                                          Bu[P][k][:, :], start=(k == 0),
                                          stop=False))
                        for k in range(DCH):
                            TCH(te.matmul(bank[:, :],
                                          w_ffn[k][:, m * 128:(m + 1) * 128],
                                          xn_f[P][:, k * TC:(k + 1) * TC],
                                          start=False,
                                          stop=(k == DCH - 1)))
                        mark(("outmm", cO * 8 + m), TCH)
                    for t in range(NTT):
                        g = cO * NTT + t
                        pump()
                        if g >= 1:
                            wt(te, "a", ("pto", g - 1))
                        wt(te, "a", ("aoev", cO * 8 + 7))
                        for k in range(DCH):
                            TCH(te.transpose(pTo[:, k * 128:(k + 1) * 128],
                                             ao[k][:, t * 128:(t + 1) * 128],
                                             eye16[:, :]))
                        mark(("ottr", g), TCH)
                if cP < nch:
                    P = cP % 2
                    for t in range(NTT):
                        g = cP * NTT + t
                        pump()
                        wt(te, "a", ("apply", g))
                        if g >= 1:
                            wt(te, "a", ("xnev", g - 1))
                        else:
                            wt(te, "v", ("wevac", NJOBS - 1))
                        for k in range(DCH):
                            if t % 2 == 0:
                                xsrc = xn_tm[:, k * 128:(k + 1) * 128]
                            elif k < 4:
                                xsrc = xin[P][0][:, k * 128:(k + 1) * 128]
                            else:
                                xsrc = xin[P][1][:, (k - 4) * 128:(k - 3) * 128]
                            TCH(te.transpose(pT1[:, k * 128:(k + 1) * 128],
                                             xsrc, eye16[:, :]))
                        mark(("xntr", g), TCH)
                    wt(te, "a", ("xnev", cP * NTT + NTT - 1))
                    for m in range(16):
                        gidx = cP * 33 + m
                        slot = gidx % WRING
                        bank = [pM1, pM2, pO1, pO2][m % 4]
                        pump()
                        if w == 1 and m == 0:
                            for q_ in range(4):
                                emit_ffmm4(q_)
                        _sn, _on = load_ord[(cP, m)]
                        te.wait_ge(DS[_sn], 16 * _on)
                        if m >= 4:
                            wt(te, "a", ("inev", cP * 16 + m - 4))
                        elif m < 2 and cP > 0:
                            wt(te, "a", ("xpev", (cP - 1) * 17 + 16 - m))
                        elif m >= 2 and cP == 1:
                            wt(te, "v", ("ffev", 12 + m))
                        elif m >= 2 and cP >= 2:
                            wt(te, "a", ("aoev", (cP - 2) * 8 + 4 + m))
                        for k in range(DCH):
                            TCH(te.matmul(bank[:, :],
                                          rin[slot][:, k * 128:(k + 1) * 128],
                                          xn_f[P][:, k * TC:(k + 1) * TC],
                                          start=(k == 0),
                                          stop=(k == DCH - 1)))
                        mark(("inmm", cP * 16 + m), TCH)
                        mark(("wgrp", gidx), TCH)
                    wt(te, "a", ("inev", cP * 16 + 7))

                    def emit_xpmm(m2, cP=cP, P=P):
                        gidx = cP * 33 + 16 + m2
                        slot = gidx % WRING
                        if m2 >= 9:
                            bank = [pO1, pO2, pM1, pM2][(m2 - 9) % 4]
                        else:
                            bank = [pM1, pM2][(16 + m2) % 2]
                        pump()
                        _sn, _on = load_ord[(cP, 16 + m2)]
                        te.wait_ge(DS[_sn], 16 * _on)
                        if m2 == 9:
                            wt(te, "a", ("dln", cP * 8 + 6))
                        elif m2 == 10:
                            wt(te, "a", ("dln", cP * 8 + 7))
                        elif m2 in (11, 12):
                            wt(te, "a", ("xpev", cP * 17 + m2 - 3))
                        elif m2 >= 13:
                            wt(te, "a", ("xpev", cP * 17 + m2 - 4))
                        elif m2 >= 2:
                            wt(te, "a", ("xpev", cP * 17 + m2 - 2))
                        else:
                            wt(te, "a", ("inev", cP * 16 + 14 + m2))
                        cc0, cc1 = xp_cols(m2)
                        ncols = cc1 - cc0
                        out_ap = bank[0:16, :] if m2 == 0 else bank[:, :]
                        for k in range(DCH):
                            TCH(te.matmul(out_ap,
                                          rx[slot][:, k * ncols:(k + 1) * ncols],
                                          xin[P][k][:, :], start=(k == 0),
                                          stop=(k == DCH - 1)))
                        mark(("xpmm", cP * 17 + m2), TCH)
                        mark(("wgrp", gidx), TCH)

                    for m2 in range(9):
                        emit_xpmm(m2)
                    wt(te, "a", ("xpev", cP * 17 + 0))
                    for k in range(DCH):
                        bank = [pO1, pO2][k % 2]
                        pump()
                        if cP * 8 + k >= 2:
                            wt(te, "a", ("dln", cP * 8 + k - 2))
                        if k < 2:
                            wt(te, "a", ("inev", cP * 16 + 14 + k))
                        TCH(te.matmul(bank[:, :],
                                      w_dt[:, k * 128:(k + 1) * 128],
                                      dlt16[:, :], start=True, stop=True))
                        mark(("dtmm", cP * 8 + k), TCH)
                    for m2 in range(9, 17):
                        emit_xpmm(m2)
                assert sk[0] == len(bnd) - 1, (w, sk[0], len(bnd))
                if w == 1 and cP >= nch:
                    for _ in range(16):
                        if sqi[0] < len(sq):
                            emit_sigma(sq[sqi[0]])
                            sqi[0] += 1
                    for t_ in range(NTT):
                        emit_ffmm4(t_)
                while sqi[0] < len(sq):
                    emit_sigma(sq[sqi[0]])
                    sqi[0] += 1

        @blk.vector
        def _(v):
            v_hi = [0]

            def vsw(val):
                if val > v_hi[0]:
                    v_hi[0] = val
                    v.wait_ge(VCH.sem, val)
            wt(v, "g", ("prep_gp",))
            VCH(v.tensor_tensor(eye16[:, :], y2_s[:, 0:128], y2_s[:, 128:256],
                                op=ALU.is_equal))
            mark(("eye",), VCH)
            for j, (wname, r0, nr, ncol) in enumerate(ROW_JOBS):
                v.wait_ge(DS["rw" + str(j % 5)], 16 * (j // 5 + 1))
                stg = LSTG[j % 5]
                VCH(v.tensor_copy(wst16[j % 2][0:nr, 0:ncol],
                                  stg[0:nr, 0:ncol]))
                mark(("cast", j), VCH)
                wt(v, "t", ("wtrans", j))
                rtile = r0 // 128
                if j < 33:
                    if j >= 2:
                        if j - 4 in wdt_after:
                            v.wait_ge(DS["wdt" + str(j % 4)],
                                      16 * wdt_after[j - 4])
                    if nr == 128:
                        VCH(v.tensor_copy(wstT[j % 4][:, 0:1024],
                                          pT1[:, 0:1024]))
                    else:
                        VCH(v.tensor_copy(
                            bass.AP(wstT[j % 4], 0,
                                    [[1024, 128], [128, DCH], [1, 16]]),
                            bass.AP(pT1, 0,
                                    [[1024, 128], [128, DCH], [1, 16]])))
                elif wname == "w_out":
                    for k in range(DCH):
                        VCH(v.tensor_copy(
                            w_out[k][:, rtile * 128:(rtile + 1) * 128],
                            pT1[:, k * 128:(k + 1) * 128]))
                else:
                    VCH(v.tensor_copy(w_dt[0:16, rtile * 128:(rtile + 1) * 128],
                                      pT1[0:16, 0:128]))
                mark(("wevac", j), VCH)
            def emit_ffev4(q):
                for gidx in range(4 * q, 4 * q + 4):
                    half, m = divmod(gidx, 8)
                    if gidx == 8:
                        v.wait_ge(DS["pd2"], 16)
                    wt(v, "t", ("ffmm", gidx))
                    bank = [pO1, pO2][gidx % 2]
                    w4v = xr_t[0][:, :].bitcast(F16)
                    VCH(v.tensor_copy(w4v[:, m * 256:(m + 1) * 256],
                                      bank[:, 0:256]))
                    mark(("ffev", gidx), VCH)
                    if gidx in (7, 15):
                        mark(("w4ev", gidx // 8), VCH)

            def emit_wneg():
                v.wait_ge(DS["pd2"], 32 + 16 * 16)
                for kk in range(4):
                    VCH(v.tensor_scalar(w_ffn[4 + kk][:, 0:512],
                                        w_ffn[4 + kk][:, 0:512], -1.0, None,
                                        op0=ALU.mult))
                mark(("wneg",), VCH)
            wt(v, "a", ("aexp",))
            acl = []
            for k in range(DCH):
                acl.append(VCH(v.tensor_scalar(a_cols[k][:, :], a_cols[k][:, :],
                                               -1.0, -5.0, op0=ALU.mult,
                                               op1=ALU.max)))
            for k in range(DCH):
                vsw(acl[k])
                VCH(v.tensor_scalar(a_cols[k][:, :], a_cols[k][:, :], -0.01,
                                    None, op0=ALU.min))
            mark(("aclip",), VCH)
            v.wait_ge(DS["pwf"], 16 * N_FIXED)
            VCH(v.tensor_add(bcomb[:, :], bcomb[:, :], bffn[:, :]))
            mark(("bcomb",), VCH)
            clip_n = {}
            last_clip = [0]
            ep2_n = [0]

            def emit_epi(cS, kk, P):
                if cS == 0 and kk == 0:
                    wt(v, "t", ("ffmm", 15))
                wt(v, "t", ("summ", cS * 128 + kk * NST + NST - 1))
                sbank = [pS, pS2][kk % 2]
                if ep2_n[0] > 0:
                    vsw(ep2_n[0])
                ey = VCH(v.tensor_mul(y_s[:, :], sbank[:, :],
                                      Cf[P][kk][:, :]))
                mark(("epiy", cS * 8 + kk), VCH)
                if kk >= 1:
                    wt(v, "g", ("attn", cS * 8 + kk - 1))
                elif cS >= 1:
                    wt(v, "g", ("attn", (cS - 1) * 8 + 7))
                vsw(ey)
                ep2_n[0] = VCH(v.scalar_tensor_tensor(
                    y2_s[:, :], xin[P][kk][:, :], c_(d_cols, kk), y_s[:, :],
                    op0=ALU.mult, op1=ALU.add))
                mark(("epi2", cS * 8 + kk), VCH)

            for cS in range(nch):
                P = cS % 2
                b, ci = divmod(cS, NCHB)
                for k in range(DCH):
                    for n in range(NST):
                        gn = k * NST + n
                        G = cS * 128 + gn
                        wt(v, "a", ("da", G))
                        if G - HR >= 0:
                            wt(v, "t", ("summ", G - HR))
                            cn = clip_n.get((G - HR) // 4, 0)
                            if cn > 0:
                                vsw(cn)
                        if gn == 0 and last_clip[0] > 0:
                            vsw(last_clip[0])
                        init = 0.0 if ci == 0 else c_(carry[k], n)
                        s_n = VCH(v.tensor_tensor_scan(
                            h_sl(gn), da_r[gn % DAR][:, :], Bu[P][k][:, :],
                            init, op0=ALU.mult, op1=ALU.add))
                        mark(("scan", G), VCH)
                        if n % 4 == 3:
                            vsw(s_n)
                            base = (gn - 3) % HR
                            src = bass.AP(h_buf, base * TC + TC - 1,
                                          [[HR * TC, 128], [TC, 4]])
                            clip_n[(cS * 128 + gn - 3) // 4] = VCH(
                                v.tensor_scalar(carry[k][:, n - 3:n + 1], src,
                                                100.0, -100.0, op0=ALU.min,
                                                op1=ALU.max))
                        if cS == 0 and gn == 23:
                            for q_ in range(4):
                                emit_ffev4(q_)
                        if n == 3 and k >= 1:
                            if cS == 0 and k == 1:
                                pass
                            elif cS == 0 and k == 2:
                                emit_epi(0, 0, P)
                                emit_epi(0, 1, P)
                            else:
                                emit_epi(cS, k - 1, P)
                        if cS == 0 and gn == 96:
                            emit_wneg()
                emit_epi(cS, DCH - 1, P)
                last_clip[0] = clip_n.get((cS * 128 + 124) // 4, 0)

        @blk.scalar
        def _(sc):
            for j, (wname, r0, nr, ncol) in enumerate(ROW_JOBS):
                if j % 2 == 1:
                    if j >= 5:
                        wt(sc, "v", ("cast", j - 5))
                    stg = LSTG[j % 5]
                    sc.dma_start(out=stg[0:nr, 0:ncol],
                                 in_=dr[wname][r0:r0 + nr, :]
                                 ).then_inc(DS["rw" + str(j % 5)], 16)
            sc.wait_ge(DS["pwf"], 16 * N_FIXED)
            for k in range(DCH):
                ACH(sc.activation(a_cols[k][:, :], a_cols[k][:, :], AF.Exp))
            mark(("aexp",), ACH)
            for w in range(nch + 2):
                cO, cS, cP = w - 2, w - 1, w
                dq = list(range(DA_HEAD, 128)) if 0 <= cS < nch else []
                dqi = [0]
                ak = [0]
                bnd = BNDS[w]

                def emit_da(gn, cS=cS):
                    k, n = gn // NST, gn % NST
                    G = cS * 128 + gn
                    Ps = cS % 2
                    if n == 0:
                        wt(sc, "g", ("u", cS * 8 + k))
                    if G - DAR >= 0:
                        wt(sc, "v", ("scan", G - DAR))
                    if G == 0:
                        wt(sc, "v", ("aclip",))
                    ACH(sc.activation(da_r[gn % DAR][:, :], dlt[Ps][k][:, :],
                                      AF.Exp, scale=c_(a_cols[k], n)))
                    mark(("da", G), ACH)

                def apump(block=False, skip=False):
                    B = bnd[ak[0]]
                    if block:
                        B2 = bnd[min(ak[0] + 3, len(bnd) - 1)] + 10.0
                    else:
                        B2 = B + 10.0
                    ak[0] += 1
                    if skip:
                        return
                    while dqi[0] < len(dq):
                        gn = dq[dqi[0]]
                        if True:
                            if (gn - (3 if block else 6)) * SCAN_NS > B2:
                                break
                            q = gn - DAR
                            req = max(q - HR, 16 * (q // 16) - 1)
                            if req >= 0 and (req - 3) * SCAN_NS > B:
                                break
                        emit_da(gn)
                        dqi[0] += 1

                if cO >= 0:
                    for m in range(8):
                        apump()
                        wt(sc, "t", ("outmm", cO * 8 + m))
                        if cO >= 1 and m == 0:
                            wt(sc, "t", ("ottr", (cO - 1) * 4 + 3))
                        bank = [pO1, pO2][m % 2]
                        ACH(sc.activation(ao[m][:, :], bank[:, :], AF.Identity,
                                          bias=c_(bcomb, m)))
                        mark(("aoev", cO * 8 + m), ACH)
                    for t in range(NTT):
                        g = cO * NTT + t
                        apump()
                        wt(sc, "t", ("ottr", g))
                        if g >= 1:
                            wt(sc, "g", ("oadd", g - 1))
                        ACH(sc.activation(otm16[:, :], pTo[:, :], AF.Identity))
                        mark(("pto", g), ACH)
                if cP < nch:
                    P = cP % 2
                    for t in range(NTT):
                        g = cP * NTT + t
                        apump()
                        if t == 0 and cP >= 2:
                            wt(sc, "v", ("epi2", (cP - 2) * 8 + 2))
                            wt(sc, "g", ("u", (cP - 2) * 8 + 2))
                        sc.wait_ge(DS["ldx" + str(g % 2)], (g // 2 + 1) * 16)
                        dmp = rx[cP % 4][:, 0:1024]
                        st1 = ACH(sc.activation(dmp, x_t[g % 2][:, :],
                                                AF.Identity,
                                                accum_out=c_(sumx, t)))
                        sc.wait_ge(ACH.sem, st1)
                        ACH(sc.activation(dmp, x_t[g % 2][:, :],
                                          AF.Square,
                                          accum_out=c_(sumsq, t)))
                        mark(("stats", g), ACH)
                        wt(sc, "g", ("lnv", g))
                        l0 = ACH(sc.activation(c_(rstd, t), c_(vcol, t), AF.Ln,
                                               bias=eps_c[:, :]))
                        sc.wait_ge(ACH.sem, l0)
                        ACH(sc.activation(c_(rstd, t), c_(rstd, t), AF.Exp,
                                          scale=mhalf_c[:, :]))
                        mark(("rstd", g), ACH)
                        wt(sc, "g", ("nm", g))
                        if g >= 2:
                            wt(sc, "t", ("xntr", g - 2))
                        if t % 2 == 0:
                            ACH(sc.activation(xn_tm[:, :], x_t[g % 2][:, :],
                                              AF.Identity, bias=c_(nmr, t),
                                              scale=c_(rstd, t)))
                        else:
                            if t == 1 and cP >= 2:
                                wt(sc, "v", ("epi2", (cP - 2) * 8 + 1))
                                wt(sc, "g", ("u", (cP - 2) * 8 + 1))
                            ACH(sc.activation(xin[P][0][:, :],
                                              x_t[g % 2][:, :512],
                                              AF.Identity, bias=c_(nmr, t),
                                              scale=c_(rstd, t)))
                            ACH(sc.activation(xin[P][1][:, :],
                                              x_t[g % 2][:, 512:],
                                              AF.Identity, bias=c_(nmr, t),
                                              scale=c_(rstd, t)))
                        mark(("apply", g), ACH)
                        if t >= 1:
                            gg = g - 1
                            wt(sc, "t", ("xntr", gg))
                            if cP >= 2 and t == 1:
                                wt(sc, "t", ("outmm", (cP - 2) * 8 + 7))
                            ACH(sc.activation(
                                bass.AP(xn_f[P], (t - 1) * 128,
                                        [[DCH * TC, 128], [TC, DCH],
                                         [1, 128]]),
                                bass.AP(pT1, 0,
                                        [[1024, 128], [128, DCH], [1, 128]]),
                                AF.Identity))
                            mark(("xnev", gg), ACH)
                    g = cP * NTT + NTT - 1
                    wt(sc, "t", ("xntr", g))
                    ACH(sc.activation(
                        bass.AP(xn_f[P], (NTT - 1) * 128,
                                [[DCH * TC, 128], [TC, DCH], [1, 128]]),
                        bass.AP(pT1, 0,
                                [[1024, 128], [128, DCH], [1, 128]]),
                        AF.Identity))
                    mark(("xnev", g), ACH)
                    for m in range(16):
                        if cP == 1:
                            apump()
                        elif m % 4 == 0:
                            apump(block=True)
                        else:
                            apump(skip=True)
                        wt(sc, "t", ("inmm", cP * 16 + m))
                        if cP >= 2 and m == 0:
                            wt(sc, "t", ("xpmm", (cP - 2) * 17 + 16))
                            wt(sc, "v", ("epi2", (cP - 2) * 8 + 7))
                        if cP >= 2 and m == 8:
                            wt(sc, "g", ("attn", (cP - 2) * 8 + 7))
                        bank = [pM1, pM2, pO1, pO2][m % 4]
                        dst = xin[P][m] if m < 8 else sres[P][m - 8]
                        ACH(sc.activation(dst[:, :], bank[:, :], AF.Silu,
                                          bias=c_(bin_c, m)))
                        mark(("inev", cP * 16 + m), ACH)
                    def emit_xpev(m2, cP=cP, P=P):
                        apump()
                        wt(sc, "t", ("xpmm", cP * 17 + m2))
                        if m2 >= 9:
                            bank = [pO1, pO2, pM1, pM2][(m2 - 9) % 4]
                        else:
                            bank = [pM1, pM2][(16 + m2) % 2]
                        if m2 == 0:
                            if cP >= 1:
                                wt(sc, "t", ("dtmm", (cP - 1) * 8 + 7))
                            ACH(sc.activation(dlt16[:, :], bank[0:16, :],
                                              AF.Identity, bias=bdlt[:, 0:1]))
                        elif m2 <= 8:
                            k = m2 - 1
                            if cP >= 2 and m2 == 1:
                                wt(sc, "t", ("outmm", (cP - 2) * 8 + 7))
                            ACH(sc.activation(Bu[P][k][:, :], bank[:, :],
                                              AF.Identity, bias=c_(bB, k)))
                        else:
                            k = m2 - 9
                            if cP >= 2 and m2 == 9:
                                wt(sc, "v", ("epiy", (cP - 2) * 8 + 7))
                            ACH(sc.activation(Cf[P][k][:, :], bank[:, :],
                                              AF.Identity, bias=c_(bC, k)))
                        mark(("xpev", cP * 17 + m2), ACH)

                    for m2 in range(9):
                        emit_xpev(m2)
                    for k in range(0, DCH, 2):
                        apump()
                        wt(sc, "t", ("dtmm", cP * 8 + k + 1))
                        if cP >= 2:
                            wt(sc, "g", ("dclip", (cP - 2) * 8 + k + 1))
                        bkA = [pO1, pO2][k % 2]
                        bkB = [pO1, pO2][(k + 1) % 2]
                        e0 = ACH(sc.activation(bkA[:, :], bkA[:, :], AF.Exp,
                                               bias=c_(bdt, k)))
                        mark(("dabs", cP * 8 + k), ACH)
                        e1 = ACH(sc.activation(bkB[:, :], bkB[:, :], AF.Exp,
                                               bias=c_(bdt, k + 1)))
                        mark(("dabs", cP * 8 + k + 1), ACH)
                        sc.wait_ge(ACH.sem, e0)
                        apump()
                        ACH(sc.activation(dlt[P][k][:, :], bkA[:, :], AF.Ln,
                                          bias=1.0))
                        mark(("dln", cP * 8 + k), ACH)
                        sc.wait_ge(ACH.sem, e1)
                        ACH(sc.activation(dlt[P][k + 1][:, :], bkB[:, :],
                                          AF.Ln, bias=1.0))
                        mark(("dln", cP * 8 + k + 1), ACH)
                    for m2 in range(9, 17):
                        emit_xpev(m2)
                assert ak[0] == len(bnd) - 1, (w, ak[0], len(bnd))
                while dqi[0] < len(dq):
                    emit_da(dq[dqi[0]])
                    dqi[0] += 1
                if 0 <= w < nch:
                    for gn_h in range(DA_HEAD):
                        emit_da(gn_h, cS=w)

        return nc


def kernel(**inputs):
    x = np.ascontiguousarray(np.asarray(inputs["L0_attn_x"], dtype=np.float32))
    shards = np.split(x, NCORES, axis=0)
    common = {
        "w_in": np.ascontiguousarray(np.asarray(inputs["L0_attn_in_proj_w"], np.float32)),
        "b_in": np.ascontiguousarray(np.asarray(inputs["L0_attn_in_proj_b"], np.float32)),
        "w_x": np.ascontiguousarray(np.asarray(inputs["L0_attn_x_proj_w"], np.float32)),
        "b_x": np.ascontiguousarray(np.asarray(inputs["L0_attn_x_proj_b"], np.float32)),
        "w_dt": np.ascontiguousarray(np.asarray(inputs["L0_attn_dt_proj_w"], np.float32)),
        "b_dt": np.ascontiguousarray(np.asarray(inputs["L0_attn_dt_proj_b"], np.float32)),
        "a_log": np.ascontiguousarray(np.asarray(inputs["L0_attn_A_log"], np.float32)),
        "d_par": np.ascontiguousarray(np.asarray(inputs["L0_attn_D_param"], np.float32)),
        "w_out": np.ascontiguousarray(np.asarray(inputs["L0_attn_out_proj_w"], np.float32)),
        "b_out": np.ascontiguousarray(np.asarray(inputs["L0_attn_out_proj_b"], np.float32)),
        "g1r": np.ascontiguousarray(np.asarray(inputs["L0_ffn_G1_R"], np.float32)),
        "g2r": np.ascontiguousarray(np.asarray(inputs["L0_ffn_G2_R"], np.float32)),
        "g1i": np.ascontiguousarray(np.asarray(inputs["L0_ffn_G1_I"], np.float32)),
        "g2i": np.ascontiguousarray(np.asarray(inputs["L0_ffn_G2_I"], np.float32)),
        "bfr": np.ascontiguousarray(np.asarray(inputs["L0_ffn_bias_real"], np.float32)),
        "bfi": np.ascontiguousarray(np.asarray(inputs["L0_ffn_bias_imag"], np.float32)),
    }
    in_maps = [dict(common, x=np.ascontiguousarray(s)) for s in shards]
    M1 = {}
    build_core({}, M1)
    M2 = {}
    nc = build_core(M1, M2)
    assert M1 == M2, "two-pass build mismatch"
    res = run_bass_kernel_spmd(nc, in_maps, core_ids=list(range(NCORES)))
    return np.concatenate([r["y"] for r in res.results], axis=0)


# version 59 (final)
# speedup vs baseline: 1.0172x; 1.0172x over previous
import contextlib
import numpy as np
import concourse.bass as bass
import concourse.mybir as mybir
from concourse.bass_utils import run_bass_kernel_spmd

F32 = mybir.dt.float32
F16 = mybir.dt.float16
ALU = mybir.AluOpType
AF = mybir.ActivationFunctionType
AX = mybir.AxisListType

NCORES = 8
WT_STAT = [0, 0]
BLOC = 2
T = 2048
D = 1024
NST = 16
TC = 512
NCHB = T // TC
DCH = D // 128
NTT = TC // 128
EPS = 1e-5
DAR = 8
DA_HEAD = 16
HR = 8
WRING = 4

C_MM = 0.27
C_TR = 0.15
SCAN_NS = 1.30


class Chain:
    def __init__(self, sem):
        self.sem = sem
        self.n = 0

    def __call__(self, inst):
        inst.then_inc(self.sem, 1)
        self.n += 1
        return self.n


def build_core(M_in, M_out):
    nch = BLOC * NCHB
    nc = bass.Bass("TRN2", target_bir_lowering=False)
    dr = {}
    dr["x"] = nc.dram_tensor("x", [BLOC, T, D], F32, kind="ExternalInput").ap()
    dr["w_in"] = nc.dram_tensor("w_in", [2 * D, D], F32, kind="ExternalInput").ap()
    dr["b_in"] = nc.dram_tensor("b_in", [2 * D], F32, kind="ExternalInput").ap()
    dr["w_x"] = nc.dram_tensor("w_x", [NST + 2 * D, D], F32, kind="ExternalInput").ap()
    dr["b_x"] = nc.dram_tensor("b_x", [NST + 2 * D], F32, kind="ExternalInput").ap()
    dr["w_dt"] = nc.dram_tensor("w_dt", [D, NST], F32, kind="ExternalInput").ap()
    dr["b_dt"] = nc.dram_tensor("b_dt", [D], F32, kind="ExternalInput").ap()
    dr["a_log"] = nc.dram_tensor("a_log", [D, NST], F32, kind="ExternalInput").ap()
    dr["d_par"] = nc.dram_tensor("d_par", [D], F32, kind="ExternalInput").ap()
    dr["w_out"] = nc.dram_tensor("w_out", [D, D], F32, kind="ExternalInput").ap()
    dr["b_out"] = nc.dram_tensor("b_out", [D], F32, kind="ExternalInput").ap()
    g1r_h = nc.dram_tensor("g1r", [1, 32, 4, 32], F32, kind="ExternalInput")
    g2r_h = nc.dram_tensor("g2r", [4, 16, 1, 16], F32, kind="ExternalInput")
    g1i_h = nc.dram_tensor("g1i", [1, 32, 4, 32], F32, kind="ExternalInput")
    g2i_h = nc.dram_tensor("g2i", [4, 16, 1, 16], F32, kind="ExternalInput")
    dr["bfr"] = nc.dram_tensor("bfr", [512], F32, kind="ExternalInput").ap()
    dr["bfi"] = nc.dram_tensor("bfi", [512], F32, kind="ExternalInput").ap()
    dr["y"] = nc.dram_tensor("y", [BLOC, T, D], F32, kind="ExternalOutput").ap()
    w4 = {"r": nc.dram_tensor("w4r", [1024, 256], F16),
          "i": nc.dram_tensor("w4i", [1024, 256], F16)}
    wt_in2 = nc.dram_tensor("wt_in2", [16, 128, DCH * 128], F16)
    wt_x2 = nc.dram_tensor("wt_x2", [17, 128, DCH * 128], F16)
    GREC = 128 * DCH * 128

    es = contextlib.ExitStack()
    with es:
        def sb(name, shape, dtype):
            return es.enter_context(nc.sbuf_tensor(name, shape, dtype))

        def pst(name, shape, dtype):
            return es.enter_context(nc.psum_tensor(name, shape, dtype))

        def mksem(name):
            return es.enter_context(nc.semaphore(name))

        w_dt = sb("wdt", [NST, D], F16)
        w_out = [sb(f"wo{k}", [128, D], F16) for k in range(DCH)]
        w_ffn = [sb(f"wf{k}", [128, D], F16) for k in range(DCH)]
        rin = [sb(f"rin{s}", [128, DCH * 128], F16) for s in range(WRING)]
        rx = [sb(f"rx{s}", [128, DCH * 128], F16) for s in range(WRING)]
        bin_c = sb("bin_c", [128, 16], F32)
        bdlt = sb("bdlt", [NST, 1], F32)
        bB = sb("bB", [128, DCH], F32)
        bC = sb("bC", [128, DCH], F32)
        bdt = sb("bdt", [128, DCH], F32)
        bcomb = sb("bcomb", [128, DCH], F32)
        bffn = sb("bffn", [128, DCH], F32)
        a_cols = [sb(f"ac{k}", [128, NST], F32) for k in range(DCH)]
        d_cols = sb("d_cols", [128, DCH], F32)
        eye16 = sb("eye16", [128, 128], F16)
        eps_c = sb("eps_c", [128, 1], F32)
        mhalf_c = sb("mhalf_c", [128, 1], F32)
        x_t = [sb(f"x_t{i}", [128, D], F32) for i in range(2)]
        xr_t = [sb(f"xr_t{i}", [128, D], F32) for i in range(2)]
        xn_tm = sb("xn_tm", [128, D], F16)
        otm16 = sb("otm16", [128, D], F16)
        out_tm = [sb(f"out_tm{i}", [128, D], F32) for i in range(2)]
        xn_f = [sb(f"xnf{p}", [128, DCH * TC], F16) for p in range(2)]
        xin = [[sb(f"xin{p}_{k}", [128, TC], F16) for k in range(DCH)]
               for p in range(2)]
        sres = [[sb(f"sres{p}_{k}", [128, TC], F16) for k in range(DCH)]
                for p in range(2)]
        Bu = [[sb(f"Bu{p}_{k}", [128, TC], F16) for k in range(DCH)]
              for p in range(2)]
        Cf = [[sb(f"Cf{p}_{k}", [128, TC], F16) for k in range(DCH)]
              for p in range(2)]
        dlt = [[sb(f"dlt{p}_{k}", [128, TC], F16) for k in range(DCH)]
               for p in range(2)]
        ao = [sb(f"ao{k}", [128, TC], F16) for k in range(DCH)]
        dlt16 = sb("dlt16", [NST, TC], F16)
        da_r = [sb(f"da{r}", [128, TC], F16) for r in range(DAR)]
        h_buf = sb("h_buf", [128, HR * TC], F16)
        carry = [sb(f"cr{k}", [128, NST], F16) for k in range(DCH)]
        y_s = sb("y_s", [128, TC], F32)
        y2_s = sb("y2_s", [128, TC], F32)
        sumx = sb("sumx", [128, NTT], F32)
        sumsq = sb("sumsq", [128, NTT], F32)
        mcol = sb("mcol", [128, NTT], F32)
        vcol = sb("vcol", [128, NTT], F32)
        rstd = sb("rstd", [128, NTT], F32)
        nmr = sb("nmr", [128, NTT], F32)
        wst16 = [rx[2], rx[3]]
        wstT = [rin[0], rin[1], rin[2], rin[3]]
        LSTG = [x_t[0], x_t[1], xr_t[0], xr_t[1], out_tm[1]]

        pT1 = pst("pT1", [128, 1024], F16)
        pM1 = pst("pM1", [128, 512], F32)
        pM2 = pst("pM2", [128, 512], F32)
        pS = pst("pS", [128, 512], F32)
        pS2 = pst("pS2", [128, 512], F32)
        pO1 = pst("pO1", [128, 512], F32)
        pO2 = pst("pO2", [128, 512], F32)
        pTo = pst("pTo", [128, 1024], F16)

        dma_names = ("ldx0 ldx1 ldr0 ldr1 rw0 rw1 rw2 rw3 rw4 pwf pd2 sto "
                     "wdt0 wdt1 wdt2 wdt3 "
                     + " ".join(f"ri{s}" for s in range(WRING)) + " "
                     + " ".join(f"rb{s}" for s in range(WRING))).split()
        DS = {n: mksem(n) for n in dma_names}
        TCH = Chain(mksem("tch"))
        VCH = Chain(mksem("vch"))
        ACH = Chain(mksem("ach"))
        GCH = Chain(mksem("gch"))
        CH = {"t": TCH, "v": VCH, "a": ACH, "g": GCH}

        def mark(key, chain):
            M_out[key] = chain.n

        wait_hi = {}
        wt_stat = WT_STAT
        wt_stat[0] = wt_stat[1] = 0

        def wt(eng, chain_id, key):
            val = M_in.get(key, 0)
            if val > 0:
                wt_stat[0] += 1
                hk = (id(eng), chain_id)
                if wait_hi.get(hk, 0) >= val:
                    wt_stat[1] += 1
                    return
                wait_hi[hk] = val
                eng.wait_ge(CH[chain_id].sem, val)

        N_FIXED = 21
        ROW_JOBS = []
        for r in range(16):
            ROW_JOBS.append(("w_in", r * 128, 128, 1024))
        for r in range(16):
            ROW_JOBS.append(("w_x", r * 128, 128, 1024))
        ROW_JOBS.append(("w_x", 2048, 16, 1024))
        for r in range(8):
            ROW_JOBS.append(("w_out", r * 128, 128, 1024))
        for r in range(8):
            ROW_JOBS.append(("w_dt", r * 128, 128, 16))
        NJOBS = len(ROW_JOBS)

        def xp_cols(m2):
            if m2 == 0:
                return 0, 16
            if m2 <= 8:
                return 16 + (m2 - 1) * 128, 16 + m2 * 128
            return 1040 + (m2 - 9) * 128, 1040 + (m2 - 8) * 128

        load_ord = {}
        _lcnt = {}
        for _c in range(nch):
            for _gi in range(33):
                _gidx = _c * 33 + _gi
                _slot = _gidx % WRING
                _sem = ("ri" if _gi < 16 else "rb") + str(_slot)
                _lcnt[_sem] = _lcnt.get(_sem, 0) + 1
                load_ord[(_c, _gi)] = (_sem, _lcnt[_sem])

        def h_sl(gn):
            r = gn % HR
            return h_buf[:, r * TC:(r + 1) * TC]

        wdt_cnt = {0: 0, 1: 0, 2: 0, 3: 0}
        wdt_after = {}

        blk = es.enter_context(nc.Block())

        def c_(tile, i):
            return tile[:, i:i + 1]

        BNDS = {}
        for w_ in range(nch + 2):
            cs_ = []
            if w_ >= 2:
                cs_ += [4.32] * 8 + [1.2] * 4
            if w_ < nch:
                cs_ += ([5.6] * 4 + [2.16] * 16 + [2.16] * 9 + [1.2] * 8
                        + [2.16] * 8)
            b_ = [0.0]
            for c_v in cs_:
                b_.append(b_[-1] + c_v)
            BNDS[w_] = b_


        @blk.sync
        def _(sy):
            ctx = nc.allow_non_contiguous_dma(reason="one-time small loads")
            ctx.__enter__()
            sy.dma_start(out=bin_c[:, :],
                         in_=dr["b_in"].rearrange("(m p) -> p m", p=128)).then_inc(DS["pwf"], 16)
            sy.dma_start(out=bdlt[:, :],
                         in_=dr["b_x"][0:NST].unsqueeze(1)).then_inc(DS["pwf"], 16)
            sy.dma_start(out=bB[:, :],
                         in_=dr["b_x"][NST:NST + D].rearrange("(m p) -> p m", p=128)).then_inc(DS["pwf"], 16)
            sy.dma_start(out=bC[:, :],
                         in_=dr["b_x"][NST + D:].rearrange("(m p) -> p m", p=128)).then_inc(DS["pwf"], 16)
            sy.dma_start(out=bdt[:, :],
                         in_=dr["b_dt"].rearrange("(m p) -> p m", p=128)).then_inc(DS["pwf"], 16)
            sy.dma_start(out=bcomb[:, :],
                         in_=dr["b_out"].rearrange("(m p) -> p m", p=128)).then_inc(DS["pwf"], 16)
            sy.dma_start(out=bffn[:, 0:4],
                         in_=dr["bfr"].rearrange("(m p) -> p m", p=128)).then_inc(DS["pwf"], 16)
            sy.dma_start(out=bffn[:, 4:8],
                         in_=dr["bfi"].rearrange("(m p) -> p m", p=128)).then_inc(DS["pwf"], 16)
            sy.dma_start(out=d_cols[:, :],
                         in_=dr["d_par"].rearrange("(m p) -> p m", p=128)).then_inc(DS["pwf"], 16)
            for k in range(DCH):
                sy.dma_start(out=a_cols[k][:, :],
                             in_=dr["a_log"][k * 128:(k + 1) * 128, :]).then_inc(DS["pwf"], 16)
            for gh, off1 in ((g1r_h, 0), (g1i_h, 32 * 1024)):
                sy.dma_start(
                    out=bass.AP(out_tm[0], off1, [[1024, 4], [32, 32], [1, 32]]),
                    in_=bass.AP(gh, 0, [[32, 4], [128, 32], [1, 32]]),
                ).then_inc(DS["pwf"], 16)
            for gh, off2 in ((g2r_h, 0), (g2i_h, 32 * 512)):
                sy.dma_start(
                    out=bass.AP(y_s, off2, [[512, 4], [1, 256]]),
                    in_=bass.AP(gh, 0, [[256, 4], [1, 256]]),
                ).then_inc(DS["pwf"], 16)
            ctx.__exit__(None, None, None)
            wbctx = nc.allow_non_contiguous_dma(reason="wb group splits")
            wbctx.__enter__()

            def emit_wb(j):
                wname, r0, nr, ncol = ROW_JOBS[j]
                wt(sy, "v", ("wevac", j))
                if wname == "w_in":
                    dst = bass.AP(wt_in2, (r0 // 128) * GREC,
                                  [[1024, 128], [1, 1024]])
                    src = bass.AP(wstT[j % 4], 0, [[1024, 128], [1, 1024]])
                    sy.dma_start(out=dst, in_=src).then_inc(
                        DS["wdt" + str(j % 4)], 16)
                    wdt_cnt[j % 4] += 1
                else:
                    segs = []
                    R = r0
                    while R < r0 + nr:
                        if R < 16:
                            g, qg, ncg = 0, R, 16
                        else:
                            g = 1 + (R - 16) // 128
                            qg = (R - 16) % 128
                            ncg = 128
                        qlen = min(r0 + nr - R, ncg - qg)
                        segs.append((g, qg, R - r0, qlen, ncg))
                        R += qlen
                    for (g, qg, jq0, qlen, ncg) in segs:
                        dst = bass.AP(wt_x2, g * GREC + qg,
                                      [[1024, 128], [ncg, DCH], [1, qlen]])
                        src = bass.AP(wstT[j % 4], jq0,
                                      [[1024, 128], [128, DCH], [1, qlen]])
                        sy.dma_start(out=dst, in_=src).then_inc(
                            DS["wdt" + str(j % 4)], 16)
                        wdt_cnt[j % 4] += 1
                wdt_after[j] = wdt_cnt[j % 4]

            for j, (wname, r0, nr, ncol) in enumerate(ROW_JOBS):
                if j % 2 == 0:
                    if j >= 5:
                        wt(sy, "v", ("cast", j - 5))
                    stg = LSTG[j % 5]
                    sy.dma_start(out=stg[0:nr, 0:ncol],
                                 in_=dr[wname][r0:r0 + nr, :]
                                 ).then_inc(DS["rw" + str(j % 5)], 16)
                if 1 <= j and j - 1 < 33:
                    emit_wb(j - 1)
            def emit_store(g):
                t_ = g % NTT
                b2, ci2 = divmod(g // NTT, NCHB)
                wt(sy, "g", ("oadd", g))
                if g >= 1:
                    sy.wait_ge(DS["sto"], 16 * g)
                sy.dma_start(
                    out=dr["y"][b2, ci2 * TC + t_ * 128:ci2 * TC + (t_ + 1) * 128, :],
                    in_=out_tm[g % 2][:, :]).then_inc(DS["sto"], 16)

            for cP in range(nch):
                b, ci = divmod(cP, NCHB)
                store_t = [0]

                def maybe_store(s_now, cP=cP, store_t=store_t):
                    while (cP >= 2 and store_t[0] < NTT
                           and 16 * store_t[0] + 20 <= s_now):
                        emit_store((cP - 2) * NTT + store_t[0])
                        store_t[0] += 1

                maybe_store(36)
                for t in range(NTT):
                    g = cP * NTT + t
                    if g >= 2:
                        wt(sy, "a", ("apply", g - 2))
                    else:
                        wt(sy, "v", ("cast", NJOBS - 1))
                    sy.dma_start(
                        out=x_t[g % 2][:, :],
                        in_=dr["x"][b, ci * TC + t * 128:ci * TC + (t + 1) * 128, :],
                    ).then_inc(DS["ldx" + str(g % 2)], 16)
                for gi in range(33):
                    gidx = cP * 33 + gi
                    slot = gidx % WRING
                    if gidx >= WRING:
                        wt(sy, "t", ("wgrp", gidx - WRING))
                    else:
                        for _s in range(4):
                            sy.wait_ge(DS["wdt" + str(_s)],
                                       16 * wdt_cnt[_s])
                    if gi == 16:
                        wt(sy, "a", ("stats", cP * 4 + 3))
                    if cP >= 1:
                        off = 12 if cP >= 2 else 0
                        if gi < 16:
                            stp = off + 4 + gi
                        elif gi - 16 <= 8:
                            stp = off + 20 + (gi - 16)
                        else:
                            stp = off + 29 + 8 + (gi - 16 - 9)
                        s_idx = int((BNDS[cP][stp] - 25.0) / SCAN_NS) - 3
                        if s_idx > 0:
                            wt(sy, "v",
                               ("scan", (cP - 1) * 128 + min(127, s_idx)))
                        maybe_store(s_idx)
                    if cP == 0 and gi >= 16 and slot in (2, 3):
                        wt(sy, "t", ("wtrans", NJOBS - 1))
                    if gi < 16:
                        src = bass.AP(wt_in2, gi * GREC,
                                      [[1024, 128], [1, 1024]])
                        dst = bass.AP(rin[slot], 0, [[1024, 128], [1, 1024]])
                    else:
                        cc0, cc1 = xp_cols(gi - 16)
                        width = DCH * (cc1 - cc0)
                        src = bass.AP(wt_x2, (gi - 16) * GREC,
                                      [[1024, 128], [1, width]])
                        dst = bass.AP(rx[slot], 0, [[1024, 128], [1, width]])
                    sem = DS[load_ord[(cP, gi)][0]]
                    sy.dma_start(out=dst, in_=src).then_inc(sem, 16)
                maybe_store(999)
            for g in range(max(0, nch - 2) * NTT, nch * NTT):
                emit_store(g)

        @blk.gpsimd
        def _(gp):
            GCH(gp.iota(y2_s[:, 0:128], [[1, 128]], channel_multiplier=0,
                        allow_small_or_imprecise_dtypes=True))
            GCH(gp.iota(y2_s[:, 128:256], [[0, 128]], channel_multiplier=1,
                        allow_small_or_imprecise_dtypes=True))
            GCH(gp.memset(eps_c[:, :], EPS))
            GCH(gp.memset(mhalf_c[:, :], -0.5))
            mark(("prep_gp",), GCH)

            def emit_w4():
                w4v3 = xr_t[0][:, :].bitcast(F16).rearrange("p (r e) -> p r e", r=8)
                wt(gp, "v", ("w4ev", 0))
                gp.dma_start(out=bass.AP(w4["r"], 0, [[256, 128], [32768, 8], [1, 256]]),
                             in_=w4v3).then_inc(DS["pd2"], 16)
                wt(gp, "v", ("w4ev", 1))
                gp.dma_start(out=bass.AP(w4["i"], 0, [[256, 128], [32768, 8], [1, 256]]),
                             in_=w4v3).then_inc(DS["pd2"], 16)
                gp.wait_ge(DS["pd2"], 32)
                for kk in range(4):
                    for (src, dstk, colb) in (("r", kk, 0), ("i", kk, 512),
                                              ("i", kk + 4, 0), ("r", kk + 4, 512)):
                        in_ap = bass.AP(w4[src], kk * 2048,
                                        [[16, 128], [8192, 32], [1, 16]])
                        gp.dma_start(out=w_ffn[dstk][:, colb:colb + 512],
                                     in_=in_ap).then_inc(DS["pd2"], 16)
            for w in range(nch + 2):
                cO, cS, cP = w - 2, w - 1, w
                ev = []
                if w == 1:
                    ev.append((18.5, "w4", 0))
                if cP < nch:
                    off = 12 if w >= 2 else 0
                    for t in range(NTT):
                        ev.append((BNDS[w][off + t] + 1.0, "ln", t))
                    dtb = BNDS[w][-17]
                    for j in range(DCH // 2):
                        ev.append((dtb + 2.0 + 5.0 * j, "dfin", 2 * j))
                if 0 <= cS < nch:
                    for k in range(DCH):
                        ev.append(((16 * k + 18) * SCAN_NS, "attn", k))
                if cO >= 0:
                    for t in range(NTT):
                        ev.append((36.0 + 2.0 * t, "oadd", t))
                ev.sort()
                for _rd, kind, idx in ev:
                    if kind == "w4":
                        emit_w4()
                    elif kind == "ln":
                        t = idx
                        g = cP * NTT + t
                        wt(gp, "a", ("stats", g))
                        s3 = GCH(gp.tensor_scalar(c_(mcol, t), c_(sumx, t),
                                                  1.0 / D, None, op0=ALU.mult))
                        GCH(gp.tensor_scalar(c_(vcol, t), c_(sumsq, t),
                                             1.0 / D, None, op0=ALU.mult))
                        gp.wait_ge(GCH.sem, s3)
                        s4 = GCH(gp.tensor_mul(c_(nmr, t), c_(mcol, t),
                                               c_(mcol, t)))
                        gp.wait_ge(GCH.sem, s4)
                        GCH(gp.tensor_sub(c_(vcol, t), c_(vcol, t), c_(nmr, t)))
                        mark(("lnv", g), GCH)
                        wt(gp, "a", ("rstd", g))
                        GCH(gp.tensor_scalar(c_(nmr, t), c_(mcol, t),
                                             c_(rstd, t), -1.0, op0=ALU.mult,
                                             op1=ALU.mult))
                        mark(("nm", g), GCH)
                    elif kind == "attn":
                        k = idx
                        P = cS % 2
                        wt(gp, "v", ("epi2", cS * 8 + k))
                        GCH(gp.tensor_mul(Bu[P][k][:, :], y2_s[:, :],
                                          sres[P][k][:, :]))
                        mark(("attn", cS * 8 + k), GCH)
                    elif kind == "oadd":
                        t = idx
                        g = cO * NTT + t
                        b, ci = divmod(cO, NCHB)
                        if g >= 2:
                            wt(gp, "g", ("oadd", g - 2))
                        else:
                            gp.wait_ge(DS["pd2"], 32)
                        gp.dma_start(
                            out=xr_t[g % 2][:, :],
                            in_=dr["x"][b, ci * TC + t * 128:ci * TC + (t + 1) * 128, :],
                        ).then_inc(DS["ldr" + str(g % 2)], 16)
                        wt(gp, "a", ("pto", g))
                        gp.wait_ge(DS["ldr" + str(g % 2)], (g // 2 + 1) * 16)
                        if g >= 2:
                            gp.wait_ge(DS["sto"], (g - 1) * 16)
                        GCH(gp.tensor_add(out_tm[g % 2][:, :], otm16[:, :],
                                          xr_t[g % 2][:, :]))
                        mark(("oadd", g), GCH)
                    else:
                        k = idx
                        P = cP % 2
                        wt(gp, "a", ("dln", cP * 8 + k + 1))
                        cp0 = GCH(gp.tensor_scalar(dlt[P][k][:, :],
                                                   dlt[P][k][:, :], 0.1, 1e-5,
                                                   op0=ALU.min, op1=ALU.max))
                        cp1 = GCH(gp.tensor_scalar(dlt[P][k + 1][:, :],
                                                   dlt[P][k + 1][:, :], 0.1,
                                                   1e-5, op0=ALU.min,
                                                   op1=ALU.max))
                        gp.wait_ge(GCH.sem, cp0)
                        m0 = GCH(gp.tensor_mul(Bu[P][k][:, :], Bu[P][k][:, :],
                                               dlt[P][k][:, :]))
                        gp.wait_ge(GCH.sem, cp1)
                        m1 = GCH(gp.tensor_mul(Bu[P][k + 1][:, :],
                                               Bu[P][k + 1][:, :],
                                               dlt[P][k + 1][:, :]))
                        gp.wait_ge(GCH.sem, m0)
                        GCH(gp.tensor_mul(Bu[P][k][:, :], Bu[P][k][:, :],
                                          xin[P][k][:, :]))
                        mark(("u", cP * 8 + k), GCH)
                        mark(("dclip", cP * 8 + k), GCH)
                        gp.wait_ge(GCH.sem, m1)
                        GCH(gp.tensor_mul(Bu[P][k + 1][:, :],
                                          Bu[P][k + 1][:, :],
                                          xin[P][k + 1][:, :]))
                        mark(("u", cP * 8 + k + 1), GCH)
                        mark(("dclip", cP * 8 + k + 1), GCH)
            gp.wait_ge(DS["sto"], nch * NTT * 16)

        @blk.tensor
        def _(te):
            wt(te, "v", ("eye",))
            for j, (wname, r0, nr, ncol) in enumerate(ROW_JOBS):
                wt(te, "v", ("cast", j))
                if j >= 1:
                    wt(te, "v", ("wevac", j - 1))
                if wname == "w_dt":
                    TCH(te.transpose(pT1[0:16, 0:128],
                                     wst16[j % 2][0:128, 0:16], eye16[:, :]))
                elif nr == 128:
                    for k in range(DCH):
                        TCH(te.transpose(pT1[:, k * 128:(k + 1) * 128],
                                         wst16[j % 2][:, k * 128:(k + 1) * 128],
                                         eye16[:, :]))
                else:
                    for k in range(DCH):
                        TCH(te.transpose(pT1[:, k * 128:k * 128 + 16],
                                         wst16[j % 2][0:16, k * 128:(k + 1) * 128],
                                         eye16[0:16, 0:16]))
                mark(("wtrans", j), TCH)
            te.wait_ge(DS["pwf"], 16 * N_FIXED)

            def emit_ffmm4(t_):
                for gidx in range(4 * t_, 4 * t_ + 4):
                    half, m = divmod(gidx, 8)
                    if gidx >= 2:
                        wt(te, "v", ("ffev", gidx - 2))
                    else:
                        wt(te, "a", ("xpev", 13 + gidx))
                    bank = [pO1, pO2][gidx % 2]
                    TCH(te.matmul(
                        bank[:, 0:256],
                        bass.AP(out_tm[0], half * 32 * 1024 + m * 128, [[1024, 4], [1, 128]]),
                        bass.AP(y_s, half * 32 * 512, [[512, 4], [1, 16], [16, 16]]),
                        start=True, stop=True))
                    mark(("ffmm", gidx), TCH)
            for w in range(nch + 2):
                cO, cS, cP = w - 2, w - 1, w
                sq = list(range(128)) if 0 <= cS < nch else []
                sqi = [0]
                sk = [0]
                bnd = BNDS[w]

                def emit_sigma(gn, cS=cS):
                    k, n = gn // NST, gn % NST
                    sbank = [pS, pS2][k % 2]
                    wt(te, "v", ("scan", cS * 128 + gn))
                    if n == 0 and cS * 8 + k >= 2:
                        wt(te, "v", ("epiy", cS * 8 + k - 2))
                    TCH(te.matmul(sbank[:, :], eye16[:, :], h_sl(gn),
                                  start=(n == 0), stop=(n == NST - 1)))
                    mark(("summ", cS * 128 + gn), TCH)

                def pump(_cost=None):
                    B = bnd[sk[0]]
                    sk[0] += 1
                    while sqi[0] < len(sq):
                        gn = sq[sqi[0]]
                        if (gn - 3) * SCAN_NS > B:
                            break
                        emit_sigma(gn)
                        sqi[0] += 1

                if cO >= 0:
                    P = cO % 2
                    for m in range(8):
                        bank = [pO1, pO2][m % 2]
                        pump()
                        if m == 0:
                            wt(te, "g", ("attn", cO * 8 + 7))
                        if m < 2:
                            wt(te, "a", ("xpev", (cO + 1) * 17 + 13 + m))
                        else:
                            wt(te, "a", ("aoev", cO * 8 + m - 2))
                        if cO == 0 and m < 2:
                            wt(te, "v", ("wneg",))
                        for k in range(DCH):
                            TCH(te.matmul(bank[:, :],
                                          w_out[k][:, m * 128:(m + 1) * 128],
                                          Bu[P][k][:, :], start=(k == 0),
                                          stop=False))
                        for k in range(DCH):
                            TCH(te.matmul(bank[:, :],
                                          w_ffn[k][:, m * 128:(m + 1) * 128],
                                          xn_f[P][:, k * TC:(k + 1) * TC],
                                          start=False,
                                          stop=(k == DCH - 1)))
                        mark(("outmm", cO * 8 + m), TCH)
                    for t in range(NTT):
                        g = cO * NTT + t
                        pump()
                        if g >= 1:
                            wt(te, "a", ("pto", g - 1))
                        wt(te, "a", ("aoev", cO * 8 + 7))
                        for k in range(DCH):
                            TCH(te.transpose(pTo[:, k * 128:(k + 1) * 128],
                                             ao[k][:, t * 128:(t + 1) * 128],
                                             eye16[:, :]))
                        mark(("ottr", g), TCH)
                if cP < nch:
                    P = cP % 2
                    for t in range(NTT):
                        g = cP * NTT + t
                        pump()
                        wt(te, "a", ("apply", g))
                        if g >= 1:
                            wt(te, "a", ("xnev", g - 1))
                        else:
                            wt(te, "v", ("wevac", NJOBS - 1))
                        for k in range(DCH):
                            if t % 2 == 0:
                                xsrc = xn_tm[:, k * 128:(k + 1) * 128]
                            elif k < 4:
                                xsrc = xin[P][0][:, k * 128:(k + 1) * 128]
                            else:
                                xsrc = xin[P][1][:, (k - 4) * 128:(k - 3) * 128]
                            TCH(te.transpose(pT1[:, k * 128:(k + 1) * 128],
                                             xsrc, eye16[:, :]))
                        mark(("xntr", g), TCH)
                    wt(te, "a", ("xnev", cP * NTT + NTT - 1))
                    for m in range(16):
                        gidx = cP * 33 + m
                        slot = gidx % WRING
                        bank = [pM1, pM2, pO1, pO2][m % 4]
                        pump()
                        if w == 1 and m == 0:
                            for q_ in range(4):
                                emit_ffmm4(q_)
                        _sn, _on = load_ord[(cP, m)]
                        te.wait_ge(DS[_sn], 16 * _on)
                        if m >= 4:
                            wt(te, "a", ("inev", cP * 16 + m - 4))
                        elif m < 2 and cP > 0:
                            wt(te, "a", ("xpev", (cP - 1) * 17 + 16 - m))
                        elif m >= 2 and cP == 1:
                            wt(te, "v", ("ffev", 12 + m))
                        elif m >= 2 and cP >= 2:
                            wt(te, "a", ("aoev", (cP - 2) * 8 + 4 + m))
                        for k in range(DCH):
                            TCH(te.matmul(bank[:, :],
                                          rin[slot][:, k * 128:(k + 1) * 128],
                                          xn_f[P][:, k * TC:(k + 1) * TC],
                                          start=(k == 0),
                                          stop=(k == DCH - 1)))
                        mark(("inmm", cP * 16 + m), TCH)
                        mark(("wgrp", gidx), TCH)
                    wt(te, "a", ("inev", cP * 16 + 7))

                    def emit_xpmm(m2, cP=cP, P=P):
                        gidx = cP * 33 + 16 + m2
                        slot = gidx % WRING
                        if m2 >= 9:
                            bank = [pO1, pO2, pM1, pM2][(m2 - 9) % 4]
                        else:
                            bank = [pM1, pM2][(16 + m2) % 2]
                        pump()
                        _sn, _on = load_ord[(cP, 16 + m2)]
                        te.wait_ge(DS[_sn], 16 * _on)
                        if m2 == 9:
                            wt(te, "a", ("dln", cP * 8 + 6))
                        elif m2 == 10:
                            wt(te, "a", ("dln", cP * 8 + 7))
                        elif m2 in (11, 12):
                            wt(te, "a", ("xpev", cP * 17 + m2 - 3))
                        elif m2 >= 13:
                            wt(te, "a", ("xpev", cP * 17 + m2 - 4))
                        elif m2 >= 2:
                            wt(te, "a", ("xpev", cP * 17 + m2 - 2))
                        else:
                            wt(te, "a", ("inev", cP * 16 + 14 + m2))
                        cc0, cc1 = xp_cols(m2)
                        ncols = cc1 - cc0
                        out_ap = bank[0:16, :] if m2 == 0 else bank[:, :]
                        for k in range(DCH):
                            TCH(te.matmul(out_ap,
                                          rx[slot][:, k * ncols:(k + 1) * ncols],
                                          xin[P][k][:, :], start=(k == 0),
                                          stop=(k == DCH - 1)))
                        mark(("xpmm", cP * 17 + m2), TCH)
                        mark(("wgrp", gidx), TCH)

                    for m2 in range(9):
                        emit_xpmm(m2)
                    wt(te, "a", ("xpev", cP * 17 + 0))
                    for k in range(DCH):
                        bank = [pO1, pO2][k % 2]
                        pump()
                        if cP * 8 + k >= 2:
                            wt(te, "a", ("dln", cP * 8 + k - 2))
                        if k < 2:
                            wt(te, "a", ("inev", cP * 16 + 14 + k))
                        TCH(te.matmul(bank[:, :],
                                      w_dt[:, k * 128:(k + 1) * 128],
                                      dlt16[:, :], start=True, stop=True))
                        mark(("dtmm", cP * 8 + k), TCH)
                    for m2 in range(9, 17):
                        emit_xpmm(m2)
                assert sk[0] == len(bnd) - 1, (w, sk[0], len(bnd))
                if w == 1 and cP >= nch:
                    for _ in range(16):
                        if sqi[0] < len(sq):
                            emit_sigma(sq[sqi[0]])
                            sqi[0] += 1
                    for t_ in range(NTT):
                        emit_ffmm4(t_)
                while sqi[0] < len(sq):
                    emit_sigma(sq[sqi[0]])
                    sqi[0] += 1

        @blk.vector
        def _(v):
            v_hi = [0]

            def vsw(val):
                if val > v_hi[0]:
                    v_hi[0] = val
                    v.wait_ge(VCH.sem, val)
            wt(v, "g", ("prep_gp",))
            VCH(v.tensor_tensor(eye16[:, :], y2_s[:, 0:128], y2_s[:, 128:256],
                                op=ALU.is_equal))
            mark(("eye",), VCH)
            for j, (wname, r0, nr, ncol) in enumerate(ROW_JOBS):
                v.wait_ge(DS["rw" + str(j % 5)], 16 * (j // 5 + 1))
                stg = LSTG[j % 5]
                VCH(v.tensor_copy(wst16[j % 2][0:nr, 0:ncol],
                                  stg[0:nr, 0:ncol]))
                mark(("cast", j), VCH)
                wt(v, "t", ("wtrans", j))
                rtile = r0 // 128
                if j < 33:
                    if j >= 2:
                        if j - 4 in wdt_after:
                            v.wait_ge(DS["wdt" + str(j % 4)],
                                      16 * wdt_after[j - 4])
                    if nr == 128:
                        VCH(v.tensor_copy(wstT[j % 4][:, 0:1024],
                                          pT1[:, 0:1024]))
                    else:
                        VCH(v.tensor_copy(
                            bass.AP(wstT[j % 4], 0,
                                    [[1024, 128], [128, DCH], [1, 16]]),
                            bass.AP(pT1, 0,
                                    [[1024, 128], [128, DCH], [1, 16]])))
                elif wname == "w_out":
                    for k in range(DCH):
                        VCH(v.tensor_copy(
                            w_out[k][:, rtile * 128:(rtile + 1) * 128],
                            pT1[:, k * 128:(k + 1) * 128]))
                else:
                    VCH(v.tensor_copy(w_dt[0:16, rtile * 128:(rtile + 1) * 128],
                                      pT1[0:16, 0:128]))
                mark(("wevac", j), VCH)
            def emit_ffev4(q):
                for gidx in range(4 * q, 4 * q + 4):
                    half, m = divmod(gidx, 8)
                    if gidx == 8:
                        v.wait_ge(DS["pd2"], 16)
                    wt(v, "t", ("ffmm", gidx))
                    bank = [pO1, pO2][gidx % 2]
                    w4v = xr_t[0][:, :].bitcast(F16)
                    VCH(v.tensor_copy(w4v[:, m * 256:(m + 1) * 256],
                                      bank[:, 0:256]))
                    mark(("ffev", gidx), VCH)
                    if gidx in (7, 15):
                        mark(("w4ev", gidx // 8), VCH)

            def emit_wneg():
                v.wait_ge(DS["pd2"], 32 + 16 * 16)
                for kk in range(4):
                    VCH(v.tensor_scalar(w_ffn[4 + kk][:, 0:512],
                                        w_ffn[4 + kk][:, 0:512], -1.0, None,
                                        op0=ALU.mult))
                mark(("wneg",), VCH)
            wt(v, "a", ("aexp",))
            acl = []
            for k in range(DCH):
                acl.append(VCH(v.tensor_scalar(a_cols[k][:, :], a_cols[k][:, :],
                                               -1.0, -5.0, op0=ALU.mult,
                                               op1=ALU.max)))
            for k in range(DCH):
                vsw(acl[k])
                VCH(v.tensor_scalar(a_cols[k][:, :], a_cols[k][:, :], -0.01,
                                    None, op0=ALU.min))
            mark(("aclip",), VCH)
            v.wait_ge(DS["pwf"], 16 * N_FIXED)
            VCH(v.tensor_add(bcomb[:, :], bcomb[:, :], bffn[:, :]))
            mark(("bcomb",), VCH)
            clip_n = {}
            last_clip = [0]
            ep2_n = [0]

            def emit_epi(cS, kk, P):
                if cS == 0 and kk == 0:
                    wt(v, "t", ("ffmm", 15))
                wt(v, "t", ("summ", cS * 128 + kk * NST + NST - 1))
                sbank = [pS, pS2][kk % 2]
                if ep2_n[0] > 0:
                    vsw(ep2_n[0])
                ey = VCH(v.tensor_mul(y_s[:, :], sbank[:, :],
                                      Cf[P][kk][:, :]))
                mark(("epiy", cS * 8 + kk), VCH)
                if kk >= 1:
                    wt(v, "g", ("attn", cS * 8 + kk - 1))
                elif cS >= 1:
                    wt(v, "g", ("attn", (cS - 1) * 8 + 7))
                vsw(ey)
                ep2_n[0] = VCH(v.scalar_tensor_tensor(
                    y2_s[:, :], xin[P][kk][:, :], c_(d_cols, kk), y_s[:, :],
                    op0=ALU.mult, op1=ALU.add))
                mark(("epi2", cS * 8 + kk), VCH)

            for cS in range(nch):
                P = cS % 2
                b, ci = divmod(cS, NCHB)
                for k in range(DCH):
                    for n in range(NST):
                        gn = k * NST + n
                        G = cS * 128 + gn
                        wt(v, "a", ("da", G))
                        if G - HR >= 0:
                            wt(v, "t", ("summ", G - HR))
                            cn = clip_n.get((G - HR) // 4, 0)
                            if cn > 0:
                                vsw(cn)
                        if gn == 0 and last_clip[0] > 0:
                            vsw(last_clip[0])
                        init = 0.0 if ci == 0 else c_(carry[k], n)
                        s_n = VCH(v.tensor_tensor_scan(
                            h_sl(gn), da_r[gn % DAR][:, :], Bu[P][k][:, :],
                            init, op0=ALU.mult, op1=ALU.add))
                        mark(("scan", G), VCH)
                        if n % 4 == 3:
                            vsw(s_n)
                            base = (gn - 3) % HR
                            src = bass.AP(h_buf, base * TC + TC - 1,
                                          [[HR * TC, 128], [TC, 4]])
                            clip_n[(cS * 128 + gn - 3) // 4] = VCH(
                                v.tensor_scalar(carry[k][:, n - 3:n + 1], src,
                                                100.0, -100.0, op0=ALU.min,
                                                op1=ALU.max))
                        if cS == 0 and gn == 23:
                            for q_ in range(4):
                                emit_ffev4(q_)
                        if n == 5 and k >= 1:
                            if cS == 0 and k == 1:
                                pass
                            elif cS == 0 and k == 2:
                                emit_epi(0, 0, P)
                                emit_epi(0, 1, P)
                            else:
                                emit_epi(cS, k - 1, P)
                        if cS == 0 and gn == 96:
                            emit_wneg()
                emit_epi(cS, DCH - 1, P)
                last_clip[0] = clip_n.get((cS * 128 + 124) // 4, 0)

        @blk.scalar
        def _(sc):
            for j, (wname, r0, nr, ncol) in enumerate(ROW_JOBS):
                if j % 2 == 1:
                    if j >= 5:
                        wt(sc, "v", ("cast", j - 5))
                    stg = LSTG[j % 5]
                    sc.dma_start(out=stg[0:nr, 0:ncol],
                                 in_=dr[wname][r0:r0 + nr, :]
                                 ).then_inc(DS["rw" + str(j % 5)], 16)
            sc.wait_ge(DS["pwf"], 16 * N_FIXED)
            for k in range(DCH):
                ACH(sc.activation(a_cols[k][:, :], a_cols[k][:, :], AF.Exp))
            mark(("aexp",), ACH)
            for w in range(nch + 2):
                cO, cS, cP = w - 2, w - 1, w
                dq = list(range(DA_HEAD, 128)) if 0 <= cS < nch else []
                dqi = [0]
                ak = [0]
                bnd = BNDS[w]

                def emit_da(gn, cS=cS):
                    k, n = gn // NST, gn % NST
                    G = cS * 128 + gn
                    Ps = cS % 2
                    if n == 0:
                        wt(sc, "g", ("u", cS * 8 + k))
                    if G - DAR >= 0:
                        wt(sc, "v", ("scan", G - DAR))
                    if G == 0:
                        wt(sc, "v", ("aclip",))
                    ACH(sc.activation(da_r[gn % DAR][:, :], dlt[Ps][k][:, :],
                                      AF.Exp, scale=c_(a_cols[k], n)))
                    mark(("da", G), ACH)

                def apump(block=False, skip=False):
                    B = bnd[ak[0]]
                    if block:
                        B2 = bnd[min(ak[0] + 3, len(bnd) - 1)] + 11.0
                    else:
                        B2 = B + 11.0
                    ak[0] += 1
                    if skip:
                        return
                    while dqi[0] < len(dq):
                        gn = dq[dqi[0]]
                        if True:
                            if (gn - (3 if block else 6)) * SCAN_NS > B2:
                                break
                            q = gn - DAR
                            req = max(q - HR, 16 * (q // 16) - 1)
                            if req >= 0 and (req - 3) * SCAN_NS > B:
                                break
                        emit_da(gn)
                        dqi[0] += 1

                if cO >= 0:
                    for m in range(8):
                        apump()
                        wt(sc, "t", ("outmm", cO * 8 + m))
                        if cO >= 1 and m == 0:
                            wt(sc, "t", ("ottr", (cO - 1) * 4 + 3))
                        bank = [pO1, pO2][m % 2]
                        ACH(sc.activation(ao[m][:, :], bank[:, :], AF.Identity,
                                          bias=c_(bcomb, m)))
                        mark(("aoev", cO * 8 + m), ACH)
                    for t in range(NTT):
                        g = cO * NTT + t
                        apump()
                        wt(sc, "t", ("ottr", g))
                        if g >= 1:
                            wt(sc, "g", ("oadd", g - 1))
                        ACH(sc.activation(otm16[:, :], pTo[:, :], AF.Identity))
                        mark(("pto", g), ACH)
                if cP < nch:
                    P = cP % 2
                    for t in range(NTT):
                        g = cP * NTT + t
                        apump()
                        if t == 0 and cP >= 2:
                            wt(sc, "v", ("epi2", (cP - 2) * 8 + 2))
                            wt(sc, "g", ("u", (cP - 2) * 8 + 2))
                        sc.wait_ge(DS["ldx" + str(g % 2)], (g // 2 + 1) * 16)
                        dmp = rx[cP % 4][:, 0:1024]
                        st1 = ACH(sc.activation(dmp, x_t[g % 2][:, :],
                                                AF.Identity,
                                                accum_out=c_(sumx, t)))
                        sc.wait_ge(ACH.sem, st1)
                        ACH(sc.activation(dmp, x_t[g % 2][:, :],
                                          AF.Square,
                                          accum_out=c_(sumsq, t)))
                        mark(("stats", g), ACH)
                        wt(sc, "g", ("lnv", g))
                        l0 = ACH(sc.activation(c_(rstd, t), c_(vcol, t), AF.Ln,
                                               bias=eps_c[:, :]))
                        sc.wait_ge(ACH.sem, l0)
                        ACH(sc.activation(c_(rstd, t), c_(rstd, t), AF.Exp,
                                          scale=mhalf_c[:, :]))
                        mark(("rstd", g), ACH)
                        wt(sc, "g", ("nm", g))
                        if g >= 2:
                            wt(sc, "t", ("xntr", g - 2))
                        if t % 2 == 0:
                            ACH(sc.activation(xn_tm[:, :], x_t[g % 2][:, :],
                                              AF.Identity, bias=c_(nmr, t),
                                              scale=c_(rstd, t)))
                        else:
                            if t == 1 and cP >= 2:
                                wt(sc, "v", ("epi2", (cP - 2) * 8 + 1))
                                wt(sc, "g", ("u", (cP - 2) * 8 + 1))
                            ACH(sc.activation(xin[P][0][:, :],
                                              x_t[g % 2][:, :512],
                                              AF.Identity, bias=c_(nmr, t),
                                              scale=c_(rstd, t)))
                            ACH(sc.activation(xin[P][1][:, :],
                                              x_t[g % 2][:, 512:],
                                              AF.Identity, bias=c_(nmr, t),
                                              scale=c_(rstd, t)))
                        mark(("apply", g), ACH)
                        if t >= 1:
                            gg = g - 1
                            wt(sc, "t", ("xntr", gg))
                            if cP >= 2 and t == 1:
                                wt(sc, "t", ("outmm", (cP - 2) * 8 + 7))
                            ACH(sc.activation(
                                bass.AP(xn_f[P], (t - 1) * 128,
                                        [[DCH * TC, 128], [TC, DCH],
                                         [1, 128]]),
                                bass.AP(pT1, 0,
                                        [[1024, 128], [128, DCH], [1, 128]]),
                                AF.Identity))
                            mark(("xnev", gg), ACH)
                    g = cP * NTT + NTT - 1
                    wt(sc, "t", ("xntr", g))
                    ACH(sc.activation(
                        bass.AP(xn_f[P], (NTT - 1) * 128,
                                [[DCH * TC, 128], [TC, DCH], [1, 128]]),
                        bass.AP(pT1, 0,
                                [[1024, 128], [128, DCH], [1, 128]]),
                        AF.Identity))
                    mark(("xnev", g), ACH)
                    for m in range(16):
                        if cP == 1:
                            apump()
                        elif m % 4 == 0:
                            apump(block=True)
                        else:
                            apump(skip=True)
                        wt(sc, "t", ("inmm", cP * 16 + m))
                        if cP >= 2 and m == 0:
                            wt(sc, "t", ("xpmm", (cP - 2) * 17 + 16))
                            wt(sc, "v", ("epi2", (cP - 2) * 8 + 7))
                        if cP >= 2 and m == 8:
                            wt(sc, "g", ("attn", (cP - 2) * 8 + 7))
                        bank = [pM1, pM2, pO1, pO2][m % 4]
                        dst = xin[P][m] if m < 8 else sres[P][m - 8]
                        ACH(sc.activation(dst[:, :], bank[:, :], AF.Silu,
                                          bias=c_(bin_c, m)))
                        mark(("inev", cP * 16 + m), ACH)
                    def emit_xpev(m2, cP=cP, P=P):
                        apump()
                        wt(sc, "t", ("xpmm", cP * 17 + m2))
                        if m2 >= 9:
                            bank = [pO1, pO2, pM1, pM2][(m2 - 9) % 4]
                        else:
                            bank = [pM1, pM2][(16 + m2) % 2]
                        if m2 == 0:
                            if cP >= 1:
                                wt(sc, "t", ("dtmm", (cP - 1) * 8 + 7))
                            ACH(sc.activation(dlt16[:, :], bank[0:16, :],
                                              AF.Identity, bias=bdlt[:, 0:1]))
                        elif m2 <= 8:
                            k = m2 - 1
                            if cP >= 2 and m2 == 1:
                                wt(sc, "t", ("outmm", (cP - 2) * 8 + 7))
                            ACH(sc.activation(Bu[P][k][:, :], bank[:, :],
                                              AF.Identity, bias=c_(bB, k)))
                        else:
                            k = m2 - 9
                            if cP >= 2 and m2 == 9:
                                wt(sc, "v", ("epiy", (cP - 2) * 8 + 7))
                            ACH(sc.activation(Cf[P][k][:, :], bank[:, :],
                                              AF.Identity, bias=c_(bC, k)))
                        mark(("xpev", cP * 17 + m2), ACH)

                    for m2 in range(9):
                        emit_xpev(m2)
                    for k in range(0, DCH, 2):
                        apump()
                        wt(sc, "t", ("dtmm", cP * 8 + k + 1))
                        if cP >= 2:
                            wt(sc, "g", ("dclip", (cP - 2) * 8 + k + 1))
                        bkA = [pO1, pO2][k % 2]
                        bkB = [pO1, pO2][(k + 1) % 2]
                        e0 = ACH(sc.activation(bkA[:, :], bkA[:, :], AF.Exp,
                                               bias=c_(bdt, k)))
                        mark(("dabs", cP * 8 + k), ACH)
                        e1 = ACH(sc.activation(bkB[:, :], bkB[:, :], AF.Exp,
                                               bias=c_(bdt, k + 1)))
                        mark(("dabs", cP * 8 + k + 1), ACH)
                        sc.wait_ge(ACH.sem, e0)
                        apump()
                        ACH(sc.activation(dlt[P][k][:, :], bkA[:, :], AF.Ln,
                                          bias=1.0))
                        mark(("dln", cP * 8 + k), ACH)
                        sc.wait_ge(ACH.sem, e1)
                        ACH(sc.activation(dlt[P][k + 1][:, :], bkB[:, :],
                                          AF.Ln, bias=1.0))
                        mark(("dln", cP * 8 + k + 1), ACH)
                    for m2 in range(9, 17):
                        emit_xpev(m2)
                assert ak[0] == len(bnd) - 1, (w, ak[0], len(bnd))
                while dqi[0] < len(dq):
                    emit_da(dq[dqi[0]])
                    dqi[0] += 1
                if 0 <= w < nch:
                    for gn_h in range(DA_HEAD):
                        emit_da(gn_h, cS=w)

        return nc


def kernel(**inputs):
    x = np.ascontiguousarray(np.asarray(inputs["L0_attn_x"], dtype=np.float32))
    shards = np.split(x, NCORES, axis=0)
    common = {
        "w_in": np.ascontiguousarray(np.asarray(inputs["L0_attn_in_proj_w"], np.float32)),
        "b_in": np.ascontiguousarray(np.asarray(inputs["L0_attn_in_proj_b"], np.float32)),
        "w_x": np.ascontiguousarray(np.asarray(inputs["L0_attn_x_proj_w"], np.float32)),
        "b_x": np.ascontiguousarray(np.asarray(inputs["L0_attn_x_proj_b"], np.float32)),
        "w_dt": np.ascontiguousarray(np.asarray(inputs["L0_attn_dt_proj_w"], np.float32)),
        "b_dt": np.ascontiguousarray(np.asarray(inputs["L0_attn_dt_proj_b"], np.float32)),
        "a_log": np.ascontiguousarray(np.asarray(inputs["L0_attn_A_log"], np.float32)),
        "d_par": np.ascontiguousarray(np.asarray(inputs["L0_attn_D_param"], np.float32)),
        "w_out": np.ascontiguousarray(np.asarray(inputs["L0_attn_out_proj_w"], np.float32)),
        "b_out": np.ascontiguousarray(np.asarray(inputs["L0_attn_out_proj_b"], np.float32)),
        "g1r": np.ascontiguousarray(np.asarray(inputs["L0_ffn_G1_R"], np.float32)),
        "g2r": np.ascontiguousarray(np.asarray(inputs["L0_ffn_G2_R"], np.float32)),
        "g1i": np.ascontiguousarray(np.asarray(inputs["L0_ffn_G1_I"], np.float32)),
        "g2i": np.ascontiguousarray(np.asarray(inputs["L0_ffn_G2_I"], np.float32)),
        "bfr": np.ascontiguousarray(np.asarray(inputs["L0_ffn_bias_real"], np.float32)),
        "bfi": np.ascontiguousarray(np.asarray(inputs["L0_ffn_bias_imag"], np.float32)),
    }
    in_maps = [dict(common, x=np.ascontiguousarray(s)) for s in shards]
    M1 = {}
    build_core({}, M1)
    M2 = {}
    nc = build_core(M1, M2)
    assert M1 == M2, "two-pass build mismatch"
    res = run_bass_kernel_spmd(nc, in_maps, core_ids=list(range(NCORES)))
    return np.concatenate([r["y"] for r in res.results], axis=0)
